# version 11
# baseline (speedup 1.0000x reference)
import os
os.environ.setdefault("NEURON_SCRATCHPAD_PAGE_SIZE", "512")
import numpy as np
import ml_dtypes
import concourse.bass as bass
import concourse.mybir as mybir
from concourse.bass import IndirectOffsetOnAxis
from concourse.bass_utils import run_bass_kernel_spmd
from contextlib import ExitStack

F32 = mybir.dt.float32
BF16 = mybir.dt.bfloat16
I32 = mybir.dt.int32
AF = mybir.ActivationFunctionType
ALU = mybir.AluOpType
AX = mybir.AxisListType

B, S, H, NH, HD = 4, 2048, 2048, 16, 128
NG, HPG = 2, 8
QTOK, NT = 1024, 8
EXP, IDIM, TOPK = 32, 1024, 4
CAP = 256
NCONV = 48
USE_WBF = True
NSLOT = EXP * CAP
SCALE = float(1.0 / np.sqrt(128.0))
NEG = -1.0e30
BIG = 1.0e9


def _K(i):
    return (i + 2) // 2


_QLO = []
for _kc in range(16):
    _i0 = next(i for i in range(8) if _K(i) * 512 > _kc * 128)
    _QLO.append(_i0 * 128)

_ENG2BLK = {"pe": "tensor", "dve": "vector", "act": "scalar",
            "pool": "gpsimd", "sp": "sync"}


class Sch:
    R = {"pool": 4, "sp": 3, "cv": 4, "cw": 4}

    def __init__(self, nc, ctx):
        self.nc = nc
        sem_names = list(_ENG2BLK)
        for q, r in self.R.items():
            sem_names += ["%s@%d" % (q, k) for k in range(r)]
        self.sem = {s: ctx.enter_context(nc.semaphore("sem_" + s.replace("@", "_")))
                    for s in sem_names}
        self.cnt = {s: 0 for s in sem_names}
        self.steps = {e: [] for e in _ENG2BLK}
        self.last_dma = {}
        self.dma_idx = {}

    def add(self, eng, fn, waits=(), inc=1, sem=None):
        sem = sem or eng
        w = {}
        for t in waits:
            if t is None:
                continue
            se, sv = t
            if sv > 0 and w.get(se, 0) < sv:
                assert sv <= self.cnt[se], ("wait on future token", se, sv, self.cnt[se])
                w[se] = sv
        self.steps[eng].append((w, fn, inc, sem))
        if inc:
            self.cnt[sem] += inc
        return (sem, self.cnt[sem])

    def dma(self, q, fn, waits=(), fam=None):
        fam = fam or q
        r = self.R.get(fam, 0)
        if r:
            i = self.dma_idx.get(fam, 0)
            self.dma_idx[fam] = i + 1
            sem = "%s@%d" % (fam, i % r)
        else:
            sem = fam
        w = list(waits)
        if sem in self.last_dma:
            w.append(self.last_dma[sem])
        tok = self.add(q, fn, w, inc=16, sem=sem)
        self.last_dma[sem] = tok
        return tok

    def tok(self, eng):
        return (eng, self.cnt[eng])

    def bar(self):
        return [(s, c) for s, c in self.cnt.items()
                if not (s.startswith("cv@") or s.startswith("cw@"))]

    def emit(self, blk):
        for eng in _ENG2BLK:
            steps = self.steps[eng]
            sem_all = self.sem

            def body(e, steps=steps):
                hwm = {k: 0 for k in self.cnt}
                for w, fn, inc, sem in steps:
                    for se, sv in w.items():
                        if sv > hwm[se]:
                            e.wait_ge(sem_all[se], sv)
                            hwm[se] = sv
                    inst = fn(e)
                    if inc:
                        inst.then_inc(sem_all[sem], inc)
            getattr(blk, _ENG2BLK[eng])(body)


class TP:

    def __init__(self, S, psTa, psTb, ident):
        self.S, self.ps, self.id = S, (psTa, psTb), ident
        self.n = 0
        self.ev = [None] * 16
        self.last_ev_bank = [None, None]

    def run(self, src_ap, dst_ap, waits, eng):
        slot = self.n % 16
        bank, off = slot % 2, (slot // 2) * 128
        ps = self.ps[bank]
        w = list(waits)
        if self.last_ev_bank[bank] is not None:
            w.append(self.last_ev_bank[bank])
        if self.ev[slot] is not None:
            w.append(self.ev[slot])
        t_tp = self.S.add(
            "pe", lambda e, ps=ps, off=off, src=src_ap:
            e.transpose(ps[:, off:off + 128], src, self.id), waits=w)
        if eng == "act":
            t_ev = self.S.add(
                eng, lambda e, ps=ps, off=off, dst=dst_ap:
                e.copy(dst, ps[:, off:off + 128]), waits=[t_tp])
        else:
            t_ev = self.S.add(
                eng, lambda e, ps=ps, off=off, dst=dst_ap:
                e.tensor_copy(dst, ps[:, off:off + 128]), waits=[t_tp])
        self.ev[slot] = t_ev
        self.last_ev_bank[bank] = t_ev
        self.n += 1
        return t_ev


def _np_consts():
    gsel = np.zeros((128, 32), dtype=np.float32)
    for k in range(128):
        gsel[k, k // 4] = 1.0
    sel127 = np.zeros((32, 128), dtype=np.float32)
    sel127[:, 127] = 1.0
    tokc = ((np.arange(32)[None, :] * 128 + np.arange(128)[:, None]) // 4)
    return gsel, sel127, tokc.astype(np.int32)


def build(stage="full"):
    nc = bass.Bass("TRN2", target_bir_lowering=False)
    dram = nc.dram_tensor

    xTk = dram("xTk", [H, S], F32, kind="ExternalInput").ap()
    xTq = dram("xTq", [H, QTOK], F32, kind="ExternalInput").ap()
    xres = dram("xres", [QTOK, H], F32, kind="ExternalInput").ap()
    wT = {m: dram("w%sT" % m, [H, H], F32, kind="ExternalInput").ap()
          for m in ("q", "k", "v", "o")}
    gateT = dram("gateT", [EXP, H, IDIM], F32, kind="ExternalInput").ap()
    upT = dram("upT", [EXP, H, IDIM], F32, kind="ExternalInput").ap()
    downT = dram("downT", [EXP, IDIM, H], F32, kind="ExternalInput").ap()
    tki = dram("tki", [QTOK, TOPK], I32, kind="ExternalInput").ap()
    tkw = dram("tkw", [QTOK, TOPK], F32, kind="ExternalInput").ap()
    phi = dram("phi", [8192, TOPK], I32, kind="ExternalInput").ap()
    phw = dram("phw", [8192, TOPK], F32, kind="ExternalInput").ap()
    msk = dram("msk", [NT, 128, 512], BF16, kind="ExternalInput").ap()
    g7 = dram("g7", [1, 1], F32, kind="ExternalInput").ap()
    out = dram("out", [QTOK, H], F32, kind="ExternalOutput").ap()

    wbf = dram("wbf", [NCONV, 128, 16 * 1024], BF16, kind="Internal").ap()
    VTd = dram("VTd", [H, S], BF16, kind="Internal").ap()
    aTd = dram("aTd", [NH, 128, QTOK], BF16, kind="Internal").ap()
    x1f = dram("x1f", [QTOK, H], F32, kind="Internal").ap()
    x1b = dram("x1b", [QTOK, H], BF16, kind="Internal").ap()
    tokslots = dram("tokslots", [NSLOT], I32, kind="Internal").ap()
    wslots = dram("wslots", [NSLOT], F32, kind="Internal").ap()
    osc = dram("osc", [NSLOT, H], F32, kind="Internal").ap()

    np_gsel, np_sel127, np_tokc = _np_consts()
    Ci = {
        "idbf": nc.inline_tensor(np.eye(128, dtype=ml_dtypes.bfloat16), "c_idbf").ap(),
        "idf": nc.inline_tensor(np.eye(128, dtype=np.float32), "c_idf").ap(),
        "ut": nc.inline_tensor(np.triu(np.ones((128, 128), np.float32), 1), "c_ut").ap(),
        "gsel": nc.inline_tensor(np_gsel, "c_gsel").ap(),
        "sel127": nc.inline_tensor(np_sel127, "c_sel127").ap(),
        "onescol": nc.inline_tensor(np.ones((128, 1), np.float32), "c_onescol").ap(),
        "ones1row": nc.inline_tensor(np.ones((1, 128), np.float32), "c_ones1row").ap(),
        "capiota": nc.inline_tensor(
            (np.arange(32, dtype=np.float32) * CAP).reshape(32, 1), "c_capiota").ap(),
        "tokc": nc.inline_tensor(np_tokc, "c_tokc").ap(),
        "tok1023": nc.inline_tensor(
            np.full((32, 1), QTOK - 1, np.int32), "c_tok1023").ap(),
    }

    with ExitStack() as ctx:
        _name_n = [0]

        def sb(name, shape, dt, c=ctx):
            _name_n[0] += 1
            return c.enter_context(
                nc.sbuf_tensor("%s_%d" % (name, _name_n[0]), shape, dt))

        idbf = sb("s_idbf", [128, 128], BF16)
        idf = sb("s_idf", [128, 128], F32)
        ut = sb("s_ut", [128, 128], F32)
        gsel = sb("s_gsel", [128, 32], F32)
        sel127 = sb("s_sel127", [32, 128], F32)
        onescol = sb("s_onescol", [128, 1], F32)
        ones1row = sb("s_ones1row", [1, 128], F32)
        capiota = sb("s_capiota", [32, 1], F32)
        tokc = sb("s_tokc", [128, 32], I32)
        tok1023 = sb("s_tok1023", [32, 1], I32)
        masks = sb("s_masks", [128, NT * 512], BF16)

        idxA = sb("idxA", [32, 128], I32)
        idxAf = sb("idxAf", [32, 128], F32)
        wA = sb("wA", [32, 128], F32)
        idxT = sb("idxT", [128, 32], F32)
        wTt = sb("wTt", [128, 32], F32)
        Msb = sb("Msb", [128, 32], F32)
        Rsb = sb("Rsb", [128, 32], F32)
        PREVsb = sb("PREVsb", [128, 32], F32)
        scanb = sb("scanb", [128, 32], F32)
        rankb = sb("rankb", [128, 32], F32)
        geb = sb("geb", [128, 32], F32)
        valb = sb("valb", [128, 32], F32)
        dest = sb("dest", [128, 32], F32)
        dest_i = sb("dest_i", [128, 32], I32)
        zeros32 = sb("zeros32", [128, 32], F32)
        phA = sb("phA", [128, 256], I32)
        phAf = sb("phAf", [128, 256], F32)
        phWf = sb("phWf", [128, 256], F32)
        phT = sb("phT", [128, 256], F32)
        WRsb = sb("WRsb", [128, 32], F32)
        counts = sb("counts", [32, 1], F32)
        Wcol = sb("Wcol", [32, 1], F32)
        g7sb = sb("g7sb", [1, 1], F32)
        g7col = sb("g7col", [32, 1], F32)
        ge2 = sb("ge2", [32, 1], F32)
        dsynf = sb("dsynf", [32, 1], F32)
        dsyn_i = sb("dsyn_i", [32, 1], I32)
        slotsG = sb("slotsG", [64, 128], I32)
        slotsGf = sb("slotsGf", [64, 128], F32)
        toksTf = sb("toksTf", [128, 64], F32)
        toks_i = sb("toks_i", [128, 64], I32)
        wcolsT = sb("wcolsT", [128, 64], F32)
        zcol_f = sb("zcol_f", [128, 64], F32)
        zcol_i = sb("zcol_i", [128, 64], I32)

        negmax_c = sb("negmax_c", [128, 1], F32)
        cmx = sb("cmx", [128, 4], F32)
        bias_c = sb("bias_c", [128, 1], F32)
        den_c = sb("den_c", [128, 1], F32)
        den2_c = sb("den2_c", [128, 1], F32)
        inv_c = sb("inv_c", [128, 1], F32)
        scol = sb("scol", [128, 1], F32)
        nmean = sb("nmean", [128, 1], F32)
        sscol = sb("sscol", [128, 1], F32)
        m2col = sb("m2col", [128, 1], F32)
        varcol = sb("varcol", [128, 1], F32)
        lncol = sb("lncol", [128, 1], F32)
        rstdcol = sb("rstdcol", [128, 1], F32)
        eps_col = sb("eps_col", [128, 1], F32)
        invrow = sb("invrow", [1, 1024], F32)

        psA = ctx.enter_context(nc.psum_tensor("psA", [128, 2048], F32))
        psB = ctx.enter_context(nc.psum_tensor("psB", [128, 1024], F32))
        psTa = ctx.enter_context(nc.psum_tensor("psTa", [128, 1024], BF16))
        psTb = ctx.enter_context(nc.psum_tensor("psTb", [128, 1024], BF16))

        S_ = Sch(nc, ctx)
        blk = ctx.enter_context(nc.Block())
        reg_bnd_ns = nc.alloc_register(mybir.EngineType.Pool, "bnd_ns")
        reg_bnd_tok = nc.alloc_register(mybir.EngineType.Pool, "bnd_tok")

        def _init_regs(e):
            e.reg_mov(reg_bnd_ns, NSLOT - 1)
            return e.reg_mov(reg_bnd_tok, QTOK - 1)
        S_.add("pool", _init_regs, inc=0)
        tp = TP(S_, psTa, psTb, idbf[:])

        conv_tok = {}

        def conv_unit(u, W):
            ei, m = divmod(u, 3)
            toks = []
            for h in range(2):
                if m == 2:
                    src = downT[ei][h * 512:(h + 1) * 512, :].rearrange(
                        "(c p) h2 -> p c h2", p=128)
                    cdim = 4
                elif m == 1:
                    src = upT[ei][h * 1024:(h + 1) * 1024, :].rearrange(
                        "(c p) i -> p c i", p=128)
                    cdim = 8
                else:
                    src = gateT[ei][h * 1024:(h + 1) * 1024, :].rearrange(
                        "(c p) i -> p c i", p=128)
                    cdim = 8
                b = W["n"] % 2
                W["n"] += 1
                stg = W["stg"][:, b * 8192:(b + 1) * 8192]
                lw = [W["free"][b]] if W["free"][b] else list(W["init"])
                t_ld = S_.dma("pool", lambda e, d=stg, s=src, c=cdim:
                              e.dma_start(out=d.rearrange("p (c f) -> p c f", c=c),
                                          in_=s), waits=lw, fam="cv")
                t_st = S_.dma("pool", lambda e, u=u, h=h, s=stg: e.dma_start(
                    out=wbf[u][:, h * 8192:(h + 1) * 8192], in_=s),
                    waits=[t_ld], fam="cw")
                W["free"][b] = t_st
                toks.append(t_st)
            conv_tok[u] = toks

        t_eps = None
        for dst, key in ((idbf, "idbf"), (idf, "idf"), (ut, "ut"),
                         (gsel, "gsel"), (sel127, "sel127"),
                         (onescol, "onescol"), (ones1row, "ones1row"),
                         (capiota, "capiota"), (tokc, "tokc"),
                         (tok1023, "tok1023")):
            S_.dma("sp", lambda e, d=dst, s=Ci[key]: e.dma_start(out=d[:], in_=s))
        S_.dma("sp", lambda e: e.dma_start(
            out=masks[:].rearrange("p (i j) -> p i j", i=NT),
            in_=msk.rearrange("i p j -> p i j")))
        t_eps = S_.add("dve", lambda e: e.memset(eps_col[:], 1e-5))
        t_cl = S_.tok("sp")

        t_tki = S_.dma("sp", lambda e: e.dma_start(
            out=idxA[:].rearrange("A (pa k) -> A pa k", k=TOPK),
            in_=tki.rearrange("(A pa) k -> A pa k", pa=32)))
        t_tkw = S_.dma("sp", lambda e: e.dma_start(
            out=wA[:].rearrange("A (pa k) -> A pa k", k=TOPK),
            in_=tkw.rearrange("(A pa) k -> A pa k", pa=32)))
        t_phi = S_.dma("sp", lambda e: e.dma_start(
            out=phA[:].rearrange("p (r k) -> p r k", k=TOPK),
            in_=phi.rearrange("(p r) k -> p r k", p=128)))
        t_phw = S_.dma("sp", lambda e: e.dma_start(
            out=phWf[:].rearrange("p (r k) -> p r k", k=TOPK),
            in_=phw.rearrange("(p r) k -> p r k", p=128)))
        t_g7 = S_.dma("sp", lambda e: e.dma_start(out=g7sb[:], in_=g7))

        t_if = S_.add("dve", lambda e: e.tensor_copy(idxAf[:], idxA[:]), waits=[t_tki])
        t_pf = S_.add("dve", lambda e: e.tensor_copy(phAf[:], phA[:]), waits=[t_phi])
        t1 = S_.add("pe", lambda e: e.transpose(psB[0:128, 0:32], idxAf[:],
                                                idf[0:32, 0:32]), waits=[t_if, t_cl])
        t1e = S_.add("dve", lambda e: e.tensor_copy(idxT[:], psB[0:128, 0:32]),
                     waits=[t1])
        t2 = S_.add("pe", lambda e: e.transpose(psB[0:128, 32:64], wA[:],
                                                idf[0:32, 0:32]),
                    waits=[t_tkw, t1e])
        t2e = S_.add("dve", lambda e: e.tensor_copy(wTt[:], psB[0:128, 32:64]),
                     waits=[t2])
        t3 = S_.add("pe", lambda e: e.matmul(psB[0:32, 64:65], ones1row[0:1, 0:32],
                                             g7sb[:], start=True, stop=True),
                    waits=[t_g7, t_cl, t2e])
        t3e = S_.add("dve", lambda e: e.tensor_copy(g7col[:], psB[0:32, 64:65]),
                     waits=[t3])

        tR = None
        for e_i in range(EXP):
            tR = S_.add("dve", lambda e, v=float(e_i), e_i=e_i: e.tensor_scalar(
                Msb[:], idxT[:], v, None, ALU.is_equal, ALU.add,
                accum_out=Rsb[:, e_i:e_i + 1]), waits=[t1e, tR])
            tq = S_.add("dve", lambda e, v=float(e_i): e.scalar_tensor_tensor(
                phT[:], phAf[:], v, phWf[:], ALU.is_equal, ALU.mult),
                waits=[t_pf, t_phw, tR])
            tR = S_.add("dve", lambda e, e_i=e_i: e.reduce_sum(
                WRsb[:, e_i:e_i + 1], phT[:], axis=AX.X), waits=[tq])
        t_loop1 = tR

        t_p0_psB = S_.tok("dve")
        P0B = {}

        def emit_p0b():
            emit_p0b1()
            emit_p0b2()

        def emit_p0b1():
            t4 = S_.add("pe", lambda e: e.matmul(psB[0:128, 128:160], ut[:], Rsb[:],
                                                 start=True, stop=True),
                        waits=[t_loop1, t3e])
            t4e = S_.add("dve", lambda e: e.tensor_copy(PREVsb[:], psB[0:128, 128:160]),
                         waits=[t4])
            t5 = S_.add("pe", lambda e: e.matmul(psB[0:32, 512:513], Rsb[:], onescol[:],
                                                 start=True, stop=True), waits=[t_loop1])
            t5e = S_.add("dve", lambda e: e.tensor_copy(counts[:], psB[0:32, 512:513]),
                         waits=[t5])
            t6 = S_.add("pe", lambda e: e.matmul(psB[0:32, 520:521], WRsb[:], onescol[:],
                                                 start=True, stop=True), waits=[t_loop1])
            t6e = S_.add("dve", lambda e: e.tensor_copy(Wcol[:], psB[0:32, 520:521]),
                         waits=[t6])
            t6g = S_.add("dve", lambda e: e.tensor_mul(Wcol[:], Wcol[:], g7col[:]),
                         waits=[t6e, t3e])

            tz0 = S_.add("dve", lambda e: e.memset(zeros32[:], 0.0))
            tda = S_.add("dve", lambda e: e.memset(dest[:], 0.0))

            tL = None
            for e_i in range(EXP):
                tm = S_.add("dve", lambda e, v=float(e_i): e.tensor_single_scalar(
                    Msb[:], idxT[:], v, ALU.is_equal), waits=[t4e, tz0, tda, tL])
                ts = S_.add("dve", lambda e: e.tensor_tensor_scan(
                    scanb[:], Msb[:], zeros32[:], 0.0, ALU.add, ALU.add), waits=[tm])
                ts = S_.add("dve", lambda e, e_i=e_i: e.scalar_tensor_tensor(
                    rankb[:], scanb[:], PREVsb[:, e_i:e_i + 1], Msb[:],
                    ALU.add, ALU.subtract), waits=[ts])
                tg = S_.add("dve", lambda e: e.tensor_single_scalar(
                    geb[:], rankb[:], float(CAP), ALU.is_ge), waits=[ts])
                tv = S_.add("dve", lambda e, v=float(e_i * CAP): e.tensor_scalar_add(
                    valb[:], rankb[:], v), waits=[ts])
                tv = S_.add("dve", lambda e: e.scalar_tensor_tensor(
                    valb[:], geb[:], BIG, valb[:], ALU.mult, ALU.add), waits=[tv, tg])
                tL = S_.add("dve", lambda e: e.tensor_mul(valb[:], valb[:], Msb[:]),
                            waits=[tv])
                tL = S_.add("dve", lambda e: e.tensor_add(dest[:], dest[:], valb[:]),
                            waits=[tL])
            t_dest = S_.add("dve", lambda e: e.tensor_copy(dest_i[:], dest[:]), waits=[tL])

            tg2 = S_.add("dve", lambda e: e.tensor_single_scalar(
                ge2[:], counts[:], float(CAP), ALU.is_ge), waits=[t5e])
            td1 = S_.add("dve", lambda e: e.scalar_tensor_tensor(
                dsynf[:], ge2[:], BIG, counts[:], ALU.mult, ALU.add), waits=[tg2])
            td2 = S_.add("dve", lambda e: e.tensor_add(dsynf[:], dsynf[:], capiota[:]),
                         waits=[td1])
            td3 = S_.add("dve", lambda e: e.tensor_copy(dsyn_i[:], dsynf[:]), waits=[td2])

            tzf = S_.add("dve", lambda e: e.memset(zcol_f[:], 0.0))
            tzi = S_.add("dve", lambda e: e.memset(zcol_i[:], 0))
            tz1 = S_.dma("sp", lambda e: e.dma_start(
                out=wslots.rearrange("(p n) -> p n", p=128), in_=zcol_f[:]), waits=[tzf])
            tz2 = S_.dma("sp", lambda e: e.dma_start(
                out=tokslots.rearrange("(p n) -> p n", p=128), in_=zcol_i[:]), waits=[tzi])


            t_sc = None
            for a in range(32):
                t_sc = S_.dma("pool", lambda e, a=a: e.indirect_dma_start(
                    out=tokslots.rearrange("(n one) -> n one", one=1),
                    out_offset=IndirectOffsetOnAxis(dest_i[:, a:a + 1], 0),
                    in_=tokc[:, a:a + 1], in_offset=None,
                    bounds_check=reg_bnd_ns, oob_is_err=False),
                    waits=[t_dest, tz2, t_cl, t_sc])
                t_sc = S_.dma("pool", lambda e, a=a: e.indirect_dma_start(
                    out=wslots.rearrange("(n one) -> n one", one=1),
                    out_offset=IndirectOffsetOnAxis(dest_i[:, a:a + 1], 0),
                    in_=wTt[:, a:a + 1], in_offset=None,
                    bounds_check=reg_bnd_ns, oob_is_err=False),
                    waits=[tz1, t_sc])
            t_sc = S_.dma("pool", lambda e: e.indirect_dma_start(
                out=tokslots.rearrange("(n one) -> n one", one=1),
                out_offset=IndirectOffsetOnAxis(dsyn_i[:], 0),
                in_=tok1023[:], in_offset=None,
                bounds_check=reg_bnd_ns, oob_is_err=False), waits=[td3, t_sc])
            t_sc = S_.dma("pool", lambda e: e.indirect_dma_start(
                out=wslots.rearrange("(n one) -> n one", one=1),
                out_offset=IndirectOffsetOnAxis(dsyn_i[:], 0),
                in_=Wcol[:], in_offset=None,
                bounds_check=reg_bnd_ns, oob_is_err=False), waits=[t6g, t_sc])
            P0B["sc"] = t_sc
            P0B["t6e"] = t6e
            P0B["dest"] = t_dest
            P0B["td3"] = td3

        def emit_p0b2():
            t_sc = P0B["sc"]
            t6e = P0B["t6e"]
            t_rb1 = S_.dma("sp", lambda e: e.dma_start(
                out=slotsG[:], in_=tokslots.rearrange("(G p) -> G p", p=128)),
                waits=[t_sc])
            t_rb1f = S_.add("dve", lambda e: e.tensor_copy(slotsGf[:], slotsG[:]),
                            waits=[t_rb1])
            t_psA_ok = S_.tok("act")
            t_rt1 = S_.add("pe", lambda e: e.transpose(
                psA[0:128, 0:64], slotsGf[:], idf[0:64, 0:64]),
                waits=[t_rb1f, t6e, t_psA_ok])
            t_rt1e = S_.add("dve", lambda e: e.tensor_copy(
                toksTf[:], psA[0:128, 0:64]), waits=[t_rt1])
            t_toks = S_.add("dve", lambda e: e.tensor_copy(toks_i[:], toksTf[:]),
                            waits=[t_rt1e])
            t_rb2 = S_.dma("sp", lambda e: e.dma_start(
                out=slotsGf[:], in_=wslots.rearrange("(G p) -> G p", p=128)),
                waits=[t_sc, t_rt1])
            t_rt2 = S_.add("pe", lambda e: e.transpose(
                psA[0:128, 64:128], slotsGf[:], idf[0:64, 0:64]),
                waits=[t_rb2, t_rt1e])
            t_wcols = S_.add("dve", lambda e: e.tensor_copy(
                wcolsT[:], psA[0:128, 64:128]), waits=[t_rt2])
            P0B["toks"] = t_toks
            P0B["wcols"] = t_wcols

        if stage == "p0":
            emit_p0b()
            bar_p0 = S_.bar()
            t_outs = []
            t_outs.append(S_.dma("sp", lambda e: e.dma_start(
                out=out[0:64, 0:128], in_=slotsGf[:]), waits=bar_p0))
            t_outs.append(S_.dma("sp", lambda e: e.dma_start(
                out=out[64:192, 0:32], in_=dest[:]), waits=bar_p0))
            t_outs.append(S_.dma("sp", lambda e: e.dma_start(
                out=out[192:320, 0:64], in_=toksTf[:]), waits=bar_p0))
            t_outs.append(S_.dma("sp", lambda e: e.dma_start(
                out=out[320:448, 0:64], in_=wcolsT[:]), waits=bar_p0))
            t_outs.append(S_.dma("sp", lambda e: e.dma_start(
                out=out[448:480, 0:1], in_=Wcol[:],
                allow_slow_non_contiguous=True), waits=bar_p0))
            t_outs.append(S_.dma("sp", lambda e: e.dma_start(
                out=out[480:512, 0:1], in_=dsynf[:],
                allow_slow_non_contiguous=True), waits=bar_p0))
            t_outs.append(S_.dma("sp", lambda e: e.dma_start(
                out=out[512:640, 0:32], in_=idxT[:]), waits=bar_p0))
            t_outs.append(S_.dma("sp", lambda e: e.dma_start(
                out=out[640:672, 0:128], in_=idxAf[:]), waits=bar_p0))
            t_outs.append(S_.dma("sp", lambda e: e.dma_start(
                out=out[672:800, 0:32], in_=Rsb[:]), waits=bar_p0))
            t_outs.append(S_.dma("sp", lambda e: e.dma_start(
                out=out[800:928, 0:32], in_=PREVsb[:]), waits=bar_p0))
            S_.add("sp", lambda e: None, waits=t_outs, inc=0)
            S_.emit(blk)
            return nc

        t_atd = []
        with ExitStack() as attn_ctx:
            KT = sb("KT", [128, HPG * 2048], BF16, attn_ctx)
            QT = sb("QT", [128, HPG * 1024], BF16, attn_ctx)
            KTv = KT[:].rearrange("p (h k) -> p h k", h=HPG)
            QTv = QT[:].rearrange("p (h q) -> p h q", h=HPG)
            mskv = masks[:].rearrange("p (i j) -> p i j", i=NT)

            conv_drain = []
            for g in range((1 if stage == "qkv" else NG)):
                gbar = ([t_cl, t_p0_psB] if g == 0
                        else S_.bar() + list(conv_drain))
                with ExitStack() as p1_ctx:
                    xk = sb("xk", [128, 16 * 2048], BF16, p1_ctx)
                    xq = sb("xq", [128, 16 * 1024], BF16, p1_ctx)
                    wm = sb("wm", [128, 16 * 1024], BF16, p1_ctx)
                    VTst = sb("VTst", [128, 2048], BF16, p1_ctx)
                    xkv = xk[:].rearrange("p (c t) -> p c t", c=16)
                    xqv = xq[:].rearrange("p (c t) -> p c t", c=16)
                    wmv = wm[:].rearrange("p (c d) -> p c d", c=16)

                    t_xk = S_.dma("pool", lambda e: e.dma_start(
                        out=xkv, in_=xTk.rearrange("(c p) t -> p c t", p=128)),
                        waits=gbar)
                    t_xq = S_.dma("pool", lambda e: e.dma_start(
                        out=xqv, in_=xTq.rearrange("(c p) t -> p c t", p=128)),
                        waits=gbar)

                    t_psA_free = list(gbar)
                    t_psB_free = list(gbar)
                    t_vst = None
                    t_wm_free = None
                    mats = ("k", "q") if stage == "qkv" else ("k", "q", "v")
                    for mat in mats:
                        t_wm = S_.dma("pool", lambda e, mat=mat, g=g: e.dma_start(
                            out=wmv,
                            in_=wT[mat][:, g * 1024:(g + 1) * 1024]
                            .rearrange("(c p) d -> p c d", p=128)),
                            waits=gbar + [t_wm_free])
                        kq = 1024 if mat == "q" else 2048
                        xv = xqv if mat == "q" else xkv
                        t_x = t_xq if mat == "q" else t_xk
                        ps = psB if mat == "q" else psA
                        for hd in range(HPG):
                            def mmfn(e, ps=ps, hd=hd, kq=kq, xv=xv):
                                last = None
                                for hc in range(16):
                                    for sl in range(kq // 512):
                                        last = e.matmul(
                                            ps[:, sl * 512:(sl + 1) * 512],
                                            wmv[:, hc, hd * 128:(hd + 1) * 128],
                                            xv[:, hc, sl * 512:(sl + 1) * 512],
                                            start=(hc == 0), stop=(hc == 15))
                                return last
                            pf = t_psB_free if mat == "q" else t_psA_free
                            t_mm = S_.add("pe", mmfn, waits=[t_wm, t_x] + pf)
                            if mat == "k":
                                dst0, dst1 = KTv[:, hd, 0:1024], KTv[:, hd, 1024:2048]
                                half = 1024
                                ew = [t_mm]
                            elif mat == "v":
                                dst0, dst1 = VTst[:, 0:1024], VTst[:, 1024:2048]
                                half = 1024
                                ew = [t_mm, t_vst]
                            else:
                                dst0, dst1 = QTv[:, hd, 0:512], QTv[:, hd, 512:1024]
                                half = 512
                                ew = [t_mm]
                            te0 = S_.add("act", lambda e, ps=ps, d=dst0, h=half:
                                         e.copy(d, ps[:, 0:h]), waits=ew)
                            te1 = S_.add("act", lambda e, ps=ps, d=dst1, h=half:
                                         e.copy(d, ps[:, h:2 * h]), waits=ew)
                            if mat == "q":
                                t_psB_free = [te0, te1]
                            else:
                                t_psA_free = [te0, te1]
                            if mat == "v":
                                gh = g * HPG + hd
                                t_vst = S_.dma("sp", lambda e, gh=gh: e.dma_start(
                                    out=VTd[gh * 128:(gh + 1) * 128, :],
                                    in_=VTst[:]), waits=[te0, te1])
                        t_wm_free = S_.tok("pe")
                    t_p1 = S_.bar()
                if stage == "qkv":
                    t_outs = []
                    t_outs.append(S_.dma("pool", lambda e: e.dma_start(
                        out=out.rearrange("(a p) o -> p a o", p=128),
                        in_=KT[:].rearrange("p (a o) -> p a o", a=8)),
                        waits=t_p1))
                    S_.add("sp", lambda e: None, waits=t_outs, inc=0)
                    S_.emit(blk)
                    return nc

                if g == 0:
                    emit_p0b1()

                with ExitStack() as p2_ctx:
                    Vsb = sb("Vsb", [128, 16 * 1024], BF16, p2_ctx)
                    ET = sb("ET", [128, 16 * 1024], BF16, p2_ctx)
                    Erot = sb("Erot", [128, 2 * 2048], BF16, p2_ctx)
                    Erot2 = sb("Erot2", [128, 2 * 2048], BF16, p2_ctx)
                    at_sb = sb("at_sb", [128, 1024], BF16, p2_ctx)
                    Vv = Vsb[:].rearrange("p (kc d) -> p kc d", kc=16)
                    ETv = ET[:].rearrange("p (kc q) -> p kc q", kc=16)

                    for kc in range(16):
                        S_.dma("sp", lambda e, kc=kc, g=g: e.dma_start(
                            out=Vv[:, kc, :],
                            in_=VTd[g * 1024:(g + 1) * 1024,
                                    kc * 128:(kc + 1) * 128],
                            transpose=True), waits=t_p1)
                    t_vl = S_.tok("sp")

                    cvstg = sb("cvstg", [128, 16384], BF16, p2_ctx)
                    Wc = {"stg": cvstg[:], "n": 0, "free": [None, None],
                          "init": list(t_p1)}
                    for u in range(g * 24, min(NCONV, g * 24 + 24)):
                        conv_unit(u, Wc)
                    conv_drain[:] = [t for t in Wc["free"] if t]

                    t_exp_p7 = None
                    t_atev_pp = None
                    t_atev_p = None
                    prev_et = None
                    prev_esc = None
                    t_atd_prev = None

                    def av_group(e, hd, kcs):
                        last = None
                        for kc in kcs:
                            qlo = _QLO[kc]
                            for sl in range(2):
                                c0 = max(qlo, sl * 512)
                                c1 = (sl + 1) * 512
                                if c0 >= c1:
                                    continue
                                last = e.matmul(
                                    psB[:, c0:c1],
                                    Vv[:, kc, hd * 128:(hd + 1) * 128],
                                    ETv[:, kc, c0:c1],
                                    start=(kc == 0), stop=(kc == 15))
                        return last

                    for hd in range(HPG):
                        gh = g * HPG + hd
                        t_exp = [None] * 8
                        t_den_rd = [None] * 8
                        t_esc = [None] * 8
                        t_lastp = [None] * 8
                        t_av_last = None

                        for i in range(10):
                            if hd > 0 and i < 8:
                                t_av_last = S_.add(
                                    "pe", lambda e, hd=hd, i=i:
                                    av_group(e, hd - 1, [2 * i, 2 * i + 1]),
                                    waits=(prev_et if i == 0 else []) +
                                    ([t_atev_pp, t_vl] if i == 0 else []))
                            if i >= 2:
                                j = i - 2
                                par_j = j % 2
                                for kc in range(4 * _K(j)):
                                    tp.run(
                                        Erot2[:, par_j * 2048 + kc * 128:
                                              par_j * 2048 + (kc + 1) * 128],
                                        ETv[:, kc, j * 128:(j + 1) * 128],
                                        [t_esc[j]], ("dve", "act")[kc % 2])
                                t_lastp[j] = S_.tok("pe")
                            if i >= 8:
                                continue
                            Ki = _K(i)
                            W = Ki * 512
                            off = (i % 2) * 1024 if i < 4 else 0
                            if i < 4:
                                w_ps = [t_exp[i - 2] if i >= 2 else None]
                                if i == 0:
                                    w_ps = ([t_exp_p7] if t_exp_p7
                                            else list(t_p1))
                            elif i == 4:
                                w_ps = [t_exp[2], t_exp[3]]
                            else:
                                w_ps = [t_exp[i - 1]]
                            t_cm = [None] * Ki
                            for ks in range(Ki):
                                t_sck = S_.add(
                                    "pe", lambda e, hd=hd, i=i, ks=ks, off=off:
                                    e.matmul(
                                        psA[:, off + ks * 512:off + (ks + 1) * 512],
                                        QTv[:, hd, i * 128:(i + 1) * 128],
                                        KTv[:, hd, ks * 512:(ks + 1) * 512],
                                        start=True, stop=True),
                                    waits=w_ps if ks == 0 else [])
                                if ks == Ki - 1:
                                    t_sck = S_.add(
                                        "dve", lambda e, i=i, Ki=Ki, off=off:
                                        e.tensor_tensor(
                                            psA[:, off + (Ki - 1) * 512:
                                                off + Ki * 512],
                                            psA[:, off + (Ki - 1) * 512:
                                                off + Ki * 512],
                                            mskv[:, i, :], ALU.add),
                                        waits=[t_sck, t_cl])
                                t_cm[ks] = S_.add(
                                    "dve", lambda e, ks=ks, off=off:
                                    e.reduce_max(
                                        cmx[:, ks:ks + 1],
                                        psA[:, off + ks * 512:off + (ks + 1) * 512],
                                        axis=AX.X), waits=[t_sck])
                            t_mx = S_.add("dve", lambda e, Ki=Ki: e.reduce_max(
                                negmax_c[:], cmx[:, 0:Ki], axis=AX.X,
                                negate=True), waits=t_cm)
                            t_b = S_.add("dve", lambda e: e.tensor_scalar_mul(
                                bias_c[:], negmax_c[:], SCALE), waits=[t_mx])
                            par = i % 2
                            t_exp[i] = S_.add(
                                "act", lambda e, par=par, W=W, off=off:
                                e.activation(
                                    Erot[:, par * 2048:par * 2048 + W],
                                    psA[:, off:off + W], AF.Exp, bias=bias_c[:],
                                    scale=SCALE, accum_out=den_c[:]),
                                waits=[t_b,
                                       t_esc[i - 2] if i >= 2 else None,
                                       t_den_rd[i - 1] if i >= 1 else None])
                            t_den_rd[i] = S_.add(
                                "dve", lambda e: e.tensor_scalar_add(
                                    den2_c[:], den_c[:], 1e-8), waits=[t_exp[i]])
                            t_rc = S_.add(
                                "dve", lambda e: e.reciprocal(
                                    inv_c[:], den2_c[:]),
                                waits=[t_den_rd[i],
                                       t_esc[i - 1] if i >= 1 else None])
                            t_esc[i] = S_.add(
                                "dve", lambda e, par=par, W=W: e.tensor_scalar_mul(
                                    Erot2[:, par * 2048:par * 2048 + W],
                                    Erot[:, par * 2048:par * 2048 + W],
                                    inv_c[:]),
                                waits=[t_rc,
                                       t_lastp[i - 2] if i >= 2 else None])
                        t_exp_p7 = t_exp[7]

                        if hd > 0:
                            t_atev = S_.add("act", lambda e: e.copy(
                                at_sb[:], psB[:, 0:1024]),
                                waits=[t_av_last, t_atd_prev])
                            t_atd_prev = S_.dma(
                                "sp", lambda e, gh=gh - 1: e.dma_start(
                                    out=aTd[gh], in_=at_sb[:]), waits=[t_atev])
                            t_atd.append(t_atd_prev)
                            t_atev_pp = t_atev_p
                            t_atev_p = t_atev
                        prev_et = [S_.tok("dve"), S_.tok("act")]

                    t_av_last = S_.add(
                        "pe", lambda e: av_group(e, HPG - 1, list(range(16))),
                        waits=prev_et + [t_atev_p, t_vl, t_exp_p7])
                    t_atev = S_.add("act", lambda e: e.copy(
                        at_sb[:], psB[:, 0:1024]), waits=[t_av_last, t_atd_prev])
                    t_atd_prev = S_.dma("sp", lambda e, gh=g * HPG + HPG - 1:
                                        e.dma_start(out=aTd[gh], in_=at_sb[:]),
                                        waits=[t_atev])
                    t_atd.append(t_atd_prev)
                if g == 0:
                    emit_p0b2()
            bar_attn = S_.bar() + list(conv_drain)
            if stage == "att":
                t_outs = []
                outv8 = out.rearrange("(g p) q -> g p q", g=8)
                for hh in range(16):
                    t_outs.append(S_.dma("pool", lambda e, hh=hh: e.dma_start(
                        out=outv8[hh // 2][:, (hh % 2) * 1024:(hh % 2 + 1) * 1024],
                        in_=aTd[hh]), waits=bar_attn))
                S_.add("sp", lambda e: None, waits=t_outs, inc=0)
                S_.emit(blk)
                return nc

        with ExitStack() as p3_ctx:
            wo_bf = sb("wo_bf", [128, 16 * 2048], BF16, p3_ctx)
            atT = sb("atT", [128, 16 * 1024], BF16, p3_ctx)
            xr0 = sb("xr0", [128, 2048], F32, p3_ctx)
            xr1 = sb("xr1", [128, 2048], F32, p3_ctx)
            zf = sb("zf", [128, 2048], F32, p3_ctx)
            sq = sb("sq", [128, 2048], F32, p3_ctx)
            x1t = sb("x1t", [128, 2048], F32, p3_ctx)
            wov = wo_bf[:].rearrange("p (c o) -> p c o", c=16)
            atv = atT[:].rearrange("p (c q) -> p c q", c=16)
            xrb = (xr0, xr1)

            t_wo = S_.dma("pool", lambda e: e.dma_start(
                out=wov, in_=wT["o"].rearrange("(c p) o -> p c o", p=128)),
                waits=bar_attn)
            t_atl = S_.dma("sp", lambda e: e.dma_start(
                out=atv, in_=aTd.rearrange("h p q -> p h q")),
                waits=bar_attn + t_atd)

            t_xr = [None] * NT
            t_xr[0] = S_.dma("sp", lambda e: e.dma_start(
                out=xr0[:], in_=xres[0:128, :]), waits=bar_attn)
            t_xr[1] = S_.dma("sp", lambda e: e.dma_start(
                out=xr1[:], in_=xres[128:256, :]), waits=bar_attn)

            t_x1b = [None] * NT
            t_psA_free = list(bar_attn)
            t_x1t_free = None
            t_z_prev = None
            for t in range(NT):
                def wofn(e, t=t):
                    last = None
                    for dc in range(16):
                        for sl in range(4):
                            last = e.matmul(
                                psA[:, sl * 512:(sl + 1) * 512],
                                atv[:, dc, t * 128:(t + 1) * 128],
                                wov[:, dc, sl * 512:(sl + 1) * 512],
                                start=(dc == 0), stop=(dc == 15))
                    return last
                t_wo_mm = S_.add("pe", wofn, waits=[t_wo, t_atl] + t_psA_free)
                t_z = S_.add("dve", lambda e, t=t: e.tensor_add(
                    zf[:], psA[:, 0:2048], xrb[t % 2][:]),
                    waits=[t_wo_mm, t_xr[t], t_x1t_free])
                t_psA_free = [t_z]
                if t + 2 < NT:
                    t_xr[t + 2] = S_.dma("sp", lambda e, t=t: e.dma_start(
                        out=xrb[t % 2][:], in_=xres[(t + 2) * 128:(t + 3) * 128, :]),
                        waits=[t_z])
                ts1 = S_.add("dve", lambda e: e.reduce_sum(
                    scol[:], zf[:], axis=AX.X, negate=True), waits=[t_z])
                ts2 = S_.add("dve", lambda e: e.tensor_scalar_mul(
                    nmean[:], scol[:], 1.0 / 2048), waits=[ts1])
                tq1 = S_.add("dve", lambda e: e.tensor_mul(sq[:], zf[:], zf[:]),
                             waits=[t_z])
                tq2 = S_.add("dve", lambda e: e.reduce_sum(
                    sscol[:], sq[:], axis=AX.X), waits=[tq1])
                tm2 = S_.add("dve", lambda e: e.tensor_mul(
                    m2col[:], nmean[:], nmean[:]), waits=[ts2])
                tv1 = S_.add("dve", lambda e: e.tensor_scalar_mul(
                    varcol[:], sscol[:], 1.0 / 2048), waits=[tq2])
                tv2 = S_.add("dve", lambda e: e.tensor_sub(
                    varcol[:], varcol[:], m2col[:]), waits=[tv1, tm2])
                tl1 = S_.add("act", lambda e: e.activation(
                    lncol[:], varcol[:], AF.Ln, bias=eps_col[:]), waits=[tv2, t_eps])
                tl2 = S_.add("act", lambda e: e.activation(
                    rstdcol[:], lncol[:], AF.Exp, scale=-0.5), waits=[tl1])
                t_x1 = S_.add("dve", lambda e: e.tensor_scalar(
                    x1t[:], zf[:], nmean[:], rstdcol[:], ALU.add, ALU.mult),
                    waits=[tl2, ts2])
                tf1 = S_.dma("sp", lambda e, t=t: e.dma_start(
                    out=x1f[t * 128:(t + 1) * 128, :], in_=x1t[:]), waits=[t_x1])
                t_x1b[t] = S_.dma("pool", lambda e, t=t: e.dma_start(
                    out=x1b[t * 128:(t + 1) * 128, :], in_=x1t[:]), waits=[t_x1])
                t_x1t_free = t_x1b[t]
                t_z_prev = t_z
            bar_p3 = S_.bar()

        if stage == "x1":
            t_outs = []
            for t in range(NT):
                t_outs.append(S_.dma("sp", lambda e, t=t: e.dma_start(
                    out=out[t * 128:(t + 1) * 128, :],
                    in_=x1f[t * 128:(t + 1) * 128, :]), waits=bar_p3))
            S_.add("sp", lambda e: None, waits=t_outs, inc=0)
            S_.emit(blk)
            return nc

        with ExitStack() as p4_ctx:
            wring = sb("wring", [128, 4 * 16 * 1024], BF16, p4_ctx)
            xs = sb("xs", [128, 2 * 2048], BF16, p4_ctx)
            xsT = sb("xsT", [128, 16 * 256], BF16, p4_ctx)
            hmT = sb("hmT", [128, 8 * 256], BF16, p4_ctx)
            silu0 = sb("silu0", [128, 1024], F32, p4_ctx)
            silu1 = sb("silu1", [128, 1024], F32, p4_ctx)
            hmid = sb("hmid", [128, 1024], BF16, p4_ctx)
            o0 = sb("o0", [128, 2048], F32, p4_ctx)
            o1 = sb("o1", [128, 2048], F32, p4_ctx)
            xsv = xs[:].rearrange("p (st h2) -> p st h2", st=2)
            xsTv = xsT[:].rearrange("p (c s) -> p c s", c=16)
            hmTv = hmT[:].rearrange("p (c s) -> p c s", c=8)
            osb = (o0, o1)
            silus = (silu0, silu1)

            def rg(slot):
                return wring[:, slot * 16384:(slot + 1) * 16384].rearrange(
                    "p (c i) -> p c i", c=16)

            def rd(slot):
                return wring[:, slot * 16384:(slot + 1) * 16384].rearrange(
                    "p (c h2) -> p c h2", c=8)

            t_toks = P0B["toks"]
            t_wcols = P0B["wcols"]
            t_osc = []
            slot_free = [None] * 4
            t_hm1_prev = None
            t_si1_prev = None
            t_oev = {}
            t_xsT_done = {0: None, 1: None}

            def ring_load(u, slot):
                if USE_WBF and u < NCONV:
                    return S_.dma("pool", lambda e, u=u, slot=slot: e.dma_start(
                        out=wring[:, slot * 16384:(slot + 1) * 16384],
                        in_=wbf[u]),
                        waits=bar_p3 + [slot_free[slot]] + conv_tok[u])
                ei, m = divmod(u, 3)
                if m == 0:
                    src = gateT[ei].rearrange("(c p) i -> p c i", p=128)
                    dstv = rg(slot)
                elif m == 1:
                    src = upT[ei].rearrange("(c p) i -> p c i", p=128)
                    dstv = rg(slot)
                else:
                    src = downT[ei].rearrange("(c p) h2 -> p c h2", p=128)
                    dstv = rd(slot)
                return S_.dma("pool", lambda e, d=dstv, s=src: e.dma_start(
                    out=d, in_=s), waits=bar_p3 + [slot_free[slot]])

            for ei in range(EXP):
                sg, su, sd = (3 * ei) % 4, (3 * ei + 1) % 4, (3 * ei + 2) % 4
                t_dg = ring_load(3 * ei, sg)
                t_du = ring_load(3 * ei + 1, su)
                t_dd = ring_load(3 * ei + 2, sd)

                t_gx = [None, None]
                for st in range(2):
                    col = ei * 2 + st
                    t_gx[st] = S_.dma("pool", lambda e, st=st, col=col:
                                      e.indirect_dma_start(
                                          out=xsv[:, st, :], out_offset=None,
                                          in_=x1b, in_offset=IndirectOffsetOnAxis(
                                              toks_i[:, col:col + 1], 0),
                                          bounds_check=reg_bnd_tok,
                                          oob_is_err=False),
                                      waits=[t_toks, t_xsT_done[st]] + t_x1b)
                for st in range(2):
                    for hc in range(16):
                        tp.run(xsv[:, st, hc * 128:(hc + 1) * 128],
                               xsTv[:, hc, st * 128:(st + 1) * 128],
                               [t_gx[st]], ("dve", "act")[hc % 2])
                    t_xsT_done[st] = S_.tok("pe")
                t_xsT_w = [S_.tok("dve"), S_.tok("act")]

                def g_mm(e, st, slot):
                    last = None
                    for hc in range(16):
                        for sl in range(2):
                            last = e.matmul(
                                psB[:, sl * 512:(sl + 1) * 512],
                                xsTv[:, hc, st * 128:(st + 1) * 128],
                                rg(slot)[:, hc, sl * 512:(sl + 1) * 512],
                                start=(hc == 0), stop=(hc == 15))
                    return last

                def u_mm(e, st, slot):
                    last = None
                    for hc in range(16):
                        for sl in range(2):
                            last = e.matmul(
                                psA[:, sl * 512:(sl + 1) * 512],
                                xsTv[:, hc, st * 128:(st + 1) * 128],
                                rg(slot)[:, hc, sl * 512:(sl + 1) * 512],
                                start=(hc == 0), stop=(hc == 15))
                    return last

                oev_p = t_oev.get((ei - 1, 1), (None, None))
                t_g0 = S_.add("pe", lambda e, sg=sg: g_mm(e, 0, sg),
                              waits=t_xsT_w + [t_dg,
                                               t_si1_prev if ei else None])
                t_s0 = S_.add("act", lambda e: e.activation(
                    silu0[:], psB[:, 0:1024], AF.Silu), waits=[t_g0])
                t_u0 = S_.add("pe", lambda e, su=su: u_mm(e, 0, su),
                              waits=[t_du, t_hm1_prev, oev_p[0], oev_p[1]])
                t_g1 = S_.add("pe", lambda e, sg=sg: g_mm(e, 1, sg),
                              waits=[t_s0])
                t_s1 = S_.add("act", lambda e: e.activation(
                    silu1[:], psB[:, 0:1024], AF.Silu), waits=[t_g1])
                t_si1_prev = t_s1
                t_h0 = S_.add("dve", lambda e, col=2 * ei: e.scalar_tensor_tensor(
                    hmid[:], psA[:, 0:1024], wcolsT[:, col:col + 1],
                    silu0[:], ALU.mult, ALU.mult),
                    waits=[t_s0, t_u0, t_wcols])
                for ic in range(8):
                    tp.run(hmid[:, ic * 128:(ic + 1) * 128],
                           hmTv[:, ic, 0:128], [t_h0], ("dve", "act")[ic % 2])
                t_hmT0 = [S_.tok("pe"), S_.tok("dve"), S_.tok("act")]
                t_u1 = S_.add("pe", lambda e, su=su: u_mm(e, 1, su),
                              waits=[t_h0])
                t_h1 = S_.add("dve", lambda e, col=2 * ei + 1: e.scalar_tensor_tensor(
                    hmid[:], psA[:, 0:1024], wcolsT[:, col:col + 1],
                    silu1[:], ALU.mult, ALU.mult),
                    waits=[t_s1, t_u1, t_hmT0[1], t_hmT0[2]])
                t_hm1_prev = t_h1
                for ic in range(8):
                    tp.run(hmid[:, ic * 128:(ic + 1) * 128],
                           hmTv[:, ic, 128:256], [t_h1], ("dve", "act")[ic % 2])
                t_hmT1 = [S_.tok("pe"), S_.tok("dve"), S_.tok("act")]

                def d_mm(e, st, slot):
                    last = None
                    for ic in range(8):
                        for sl in range(4):
                            last = e.matmul(
                                psA[:, sl * 512:(sl + 1) * 512],
                                hmTv[:, ic, st * 128:(st + 1) * 128],
                                rd(slot)[:, ic, sl * 512:(sl + 1) * 512],
                                start=(ic == 0), stop=(ic == 7))
                    return last

                t_d0 = S_.add("pe", lambda e, sd=sd: d_mm(e, 0, sd),
                              waits=t_hmT0[1:] + [t_dd, t_h1,
                                                  oev_p[0], oev_p[1]])
                te0 = S_.add("dve", lambda e: e.tensor_copy(
                    o0[:, 0:1024], psA[:, 0:1024]), waits=[t_d0])
                te1 = S_.add("act", lambda e: e.copy(
                    o0[:, 1024:2048], psA[:, 1024:2048]), waits=[t_d0])
                t_osc.append(S_.dma("sp", lambda e, row0=ei * CAP: e.dma_start(
                    out=osc[row0:row0 + 128, :], in_=o0[:]), waits=[te0, te1]))
                t_d1 = S_.add("pe", lambda e, sd=sd: d_mm(e, 1, sd),
                              waits=t_hmT1[1:] + [te0, te1])
                tf0 = S_.add("dve", lambda e: e.tensor_copy(
                    o1[:, 0:1024], psA[:, 0:1024]), waits=[t_d1])
                tf1 = S_.add("act", lambda e: e.copy(
                    o1[:, 1024:2048], psA[:, 1024:2048]), waits=[t_d1])
                t_oev[(ei, 1)] = (tf0, tf1)
                t_osc.append(S_.dma("sp", lambda e, row0=ei * CAP + 128:
                                    e.dma_start(out=osc[row0:row0 + 128, :],
                                                in_=o1[:]), waits=[tf0, tf1]))
                slot_free[sg] = t_g1
                slot_free[su] = t_u1
                slot_free[sd] = t_d1
            bar_p4 = S_.bar()

        with ExitStack() as p5_ctx:
            G = [[sb("G%d_%d" % (pp, c), [128, 2048], F32, p5_ctx)
                  for c in range(4)] for pp in range(2)]
            G2 = sb("G2", [32, 2048], F32, p5_ctx)
            x1w0 = sb("x1w0", [128, 2048], F32, p5_ctx)
            x1w1 = sb("x1w1", [128, 2048], F32, p5_ctx)
            z2 = sb("z2", [128, 2048], F32, p5_ctx)
            sq2 = sb("sq2", [128, 2048], F32, p5_ctx)
            outt = sb("outt", [128, 2048], F32, p5_ctx)
            x1wb = (x1w0, x1w1)

            t_g2z = S_.add("dve", lambda e: e.memset(G2[:], 0.0), waits=bar_p4)
            t_g2 = S_.dma("pool", lambda e: e.indirect_dma_start(
                out=G2[:], out_offset=None,
                in_=osc, in_offset=IndirectOffsetOnAxis(dsyn_i[:], 0),
                bounds_check=reg_bnd_ns, oob_is_err=False),
                waits=bar_p4 + [t_g2z, P0B["td3"]] + t_osc)

            t_x1r = [None] * NT
            t_x1r[0] = S_.dma("sp", lambda e: e.dma_start(
                out=x1w0[:], in_=x1f[0:128, :]), waits=bar_p4)
            t_x1r[1] = S_.dma("sp", lambda e: e.dma_start(
                out=x1w1[:], in_=x1f[128:256, :]), waits=bar_p4)

            t_outw = []
            t_psA_free = list(bar_p4)
            t_out_prev = None
            t_o_prev = None
            t_gath = [None] * NT
            t_cmb_hist = [None] * NT

            def emit_gathers(t):
                par = t % 2
                toks = []
                prev_cmb = t_cmb_hist[t - 2] if t >= 2 else None
                for c in range(4):
                    a = t * 4 + c
                    tzg = S_.add("dve", lambda e, par=par, c=c: e.memset(
                        G[par][c][:], 0.0),
                        waits=bar_p4 + ([prev_cmb] if prev_cmb else []))
                    toks.append(S_.dma("pool", lambda e, par=par, c=c, a=a:
                                       e.indirect_dma_start(
                                           out=G[par][c][:], out_offset=None,
                                           in_=osc,
                                           in_offset=IndirectOffsetOnAxis(
                                               dest_i[:, a:a + 1], 0),
                                           bounds_check=reg_bnd_ns,
                                           oob_is_err=False),
                                       waits=[tzg, P0B["dest"]] + t_osc))
                t_gath[t] = toks

            emit_gathers(0)
            emit_gathers(1)
            for t in range(NT):
                par = t % 2

                def cmbfn(e, t=t, par=par):
                    last = None
                    for c in range(4):
                        for sl in range(4):
                            last = e.matmul(
                                psA[32 * c:32 * (c + 1), sl * 512:(sl + 1) * 512],
                                gsel[:], G[par][c][:, sl * 512:(sl + 1) * 512],
                                start=True, stop=(t != NT - 1 and c == 3),
                                tile_position=(0, 32 * c))
                    if t == NT - 1:
                        for sl in range(4):
                            last = e.matmul(
                                psA[0:128, sl * 512:(sl + 1) * 512],
                                sel127[:], G2[:, sl * 512:(sl + 1) * 512],
                                start=False, stop=True)
                    return last
                t_cmb = S_.add("pe", cmbfn,
                               waits=t_gath[t] + t_psA_free +
                               ([t_g2] if t == NT - 1 else []))
                t_cmb_hist[t] = t_cmb
                if t + 2 < NT:
                    emit_gathers(t + 2)
                t_z2 = S_.add("dve", lambda e, t=t: e.tensor_add(
                    z2[:], psA[:, 0:2048], x1wb[t % 2][:]),
                    waits=[t_cmb, t_x1r[t], t_o_prev])
                t_psA_free = [t_z2]
                if t + 2 < NT:
                    t_x1r[t + 2] = S_.dma("sp", lambda e, t=t: e.dma_start(
                        out=x1wb[t % 2][:],
                        in_=x1f[(t + 2) * 128:(t + 3) * 128, :]), waits=[t_z2])
                ts1 = S_.add("dve", lambda e: e.reduce_sum(
                    scol[:], z2[:], axis=AX.X, negate=True), waits=[t_z2])
                ts2 = S_.add("dve", lambda e: e.tensor_scalar_mul(
                    nmean[:], scol[:], 1.0 / 2048), waits=[ts1])
                tq1 = S_.add("dve", lambda e: e.tensor_mul(sq2[:], z2[:], z2[:]),
                             waits=[t_z2])
                tq2 = S_.add("dve", lambda e: e.reduce_sum(
                    sscol[:], sq2[:], axis=AX.X), waits=[tq1])
                tm2 = S_.add("dve", lambda e: e.tensor_mul(
                    m2col[:], nmean[:], nmean[:]), waits=[ts2])
                tv1 = S_.add("dve", lambda e: e.tensor_scalar_mul(
                    varcol[:], sscol[:], 1.0 / 2048), waits=[tq2])
                tv2 = S_.add("dve", lambda e: e.tensor_sub(
                    varcol[:], varcol[:], m2col[:]), waits=[tv1, tm2])
                tl1 = S_.add("act", lambda e: e.activation(
                    lncol[:], varcol[:], AF.Ln, bias=eps_col[:]), waits=[tv2, t_eps])
                tl2 = S_.add("act", lambda e: e.activation(
                    rstdcol[:], lncol[:], AF.Exp, scale=-0.5), waits=[tl1])
                t_o = S_.add("dve", lambda e: e.tensor_scalar(
                    outt[:], z2[:], nmean[:], rstdcol[:], ALU.add, ALU.mult),
                    waits=[tl2, ts2, t_out_prev])
                t_o_prev = t_o
                t_out_prev = S_.dma("sp", lambda e, t=t: e.dma_start(
                    out=out[t * 128:(t + 1) * 128, :], in_=outt[:]), waits=[t_o])
                t_outw.append(t_out_prev)

            S_.add("sp", lambda e: None, waits=t_outw, inc=0)
        S_.emit(blk)
    return nc


_CACHE = {}


def _get_nc(stage="full"):
    if stage not in _CACHE:
        _CACHE[stage] = build(stage)
    return _CACHE[stage]


def _rows_of_core(c):
    h = c % 2
    return ((2 * np.arange(8)[:, None] + h) * 128 +
            np.arange(128)[None, :]).reshape(-1)


def _make_masks(h):
    m = np.zeros((NT, 128, 512), dtype=np.float32)
    for i in range(NT):
        Ki = _K(i)
        qa = (2 * i + h) * 128
        kglob = (Ki - 1) * 512 + np.arange(512)[None, :]
        allowed = kglob <= (qa + np.arange(128)[:, None])
        m[i] = np.where(allowed, 0.0, NEG)
    return m.astype(ml_dtypes.bfloat16)


def _prep_host(inputs):
    x = np.ascontiguousarray(np.asarray(inputs["L0_attn_x"], dtype=np.float32))
    wq = np.asarray(inputs["L0_attn_wq"], dtype=np.float32)
    wk = np.asarray(inputs["L0_attn_wk"], dtype=np.float32)
    wv = np.asarray(inputs["L0_attn_wv"], dtype=np.float32)
    wo = np.asarray(inputs["L0_attn_wo"], dtype=np.float32)
    gate = np.asarray(inputs["L0_ffn_gate_weight"], dtype=np.float32)
    up = np.asarray(inputs["L0_ffn_up_weight"], dtype=np.float32)
    down = np.asarray(inputs["L0_ffn_down_weight"], dtype=np.float32)
    tidx = np.asarray(inputs["L0_ffn_topk_idx"], dtype=np.int32)
    tw = np.asarray(inputs["L0_ffn_topk_weight"], dtype=np.float32)

    shared = {
        "wqT": np.ascontiguousarray(wq.T),
        "wkT": np.ascontiguousarray(wk.T),
        "wvT": np.ascontiguousarray(wv.T),
        "woT": np.ascontiguousarray(wo.T),
        "gateT": np.ascontiguousarray(gate.transpose(0, 2, 1)),
        "upT": np.ascontiguousarray(up.transpose(0, 2, 1)),
        "downT": np.ascontiguousarray(down.transpose(0, 2, 1)),
        "phi": np.ascontiguousarray(tidx[8192:16384]),
        "phw": np.ascontiguousarray(tw[8192:16384]),
    }
    xT = np.ascontiguousarray(x.transpose(0, 2, 1))
    in_maps = []
    for c in range(8):
        b, h = c // 2, c % 2
        rows = _rows_of_core(c)
        m = dict(shared)
        m["xTk"] = xT[b]
        m["xTq"] = np.ascontiguousarray(xT[b][:, rows])
        m["xres"] = np.ascontiguousarray(x[b][rows])
        m["tki"] = np.ascontiguousarray(tidx[b * S + rows])
        m["tkw"] = np.ascontiguousarray(tw[b * S + rows])
        m["msk"] = _make_masks(h)
        m["g7"] = np.array([[1.0 if c == 7 else 0.0]], dtype=np.float32)
        in_maps.append(m)
    return in_maps


def kernel(**inputs):
    nc = _get_nc("full")
    in_maps = _prep_host(inputs)
    res = run_bass_kernel_spmd(nc, in_maps, core_ids=list(range(8)))
    outa = np.empty((B, S, H), dtype=np.float32)
    for c in range(8):
        b = c // 2
        rows = _rows_of_core(c)
        outa[b, rows] = np.asarray(res.results[c]["out"])
    return outa


# version 12 (final)
# speedup vs baseline: 1.1840x; 1.1840x over previous
import os
os.environ.setdefault("NEURON_SCRATCHPAD_PAGE_SIZE", "512")
import numpy as np
import ml_dtypes
import concourse.bass as bass
import concourse.mybir as mybir
from concourse.bass import IndirectOffsetOnAxis
from concourse.bass_utils import run_bass_kernel_spmd
from contextlib import ExitStack

F32 = mybir.dt.float32
BF16 = mybir.dt.bfloat16
I32 = mybir.dt.int32
AF = mybir.ActivationFunctionType
ALU = mybir.AluOpType
AX = mybir.AxisListType

B, S, H, NH, HD = 4, 2048, 2048, 16, 128
NG, HPG = 2, 8
QTOK, NT = 1024, 8
EXP, IDIM, TOPK = 32, 1024, 4
CAP = 256
NCONV = 21
CV_W0 = 8
USE_WBF = True
NSLOT = EXP * CAP
SCALE = float(1.0 / np.sqrt(128.0))
NEG = -1.0e30
BIG = 1.0e9


def _K(i):
    return (i + 2) // 2


_QLO = []
for _kc in range(16):
    _i0 = next(i for i in range(8) if _K(i) * 512 > _kc * 128)
    _QLO.append(_i0 * 128)

_ENG2BLK = {"pe": "tensor", "dve": "vector", "act": "scalar",
            "pool": "gpsimd", "sp": "sync"}


class Sch:
    R = {"pool": 4, "sp": 3, "cv": 4, "cw": 4}

    def __init__(self, nc, ctx):
        self.nc = nc
        sem_names = list(_ENG2BLK)
        for q, r in self.R.items():
            sem_names += ["%s@%d" % (q, k) for k in range(r)]
        self.sem = {s: ctx.enter_context(nc.semaphore("sem_" + s.replace("@", "_")))
                    for s in sem_names}
        self.cnt = {s: 0 for s in sem_names}
        self.steps = {e: [] for e in _ENG2BLK}
        self.last_dma = {}
        self.dma_idx = {}

    def add(self, eng, fn, waits=(), inc=1, sem=None):
        sem = sem or eng
        w = {}
        for t in waits:
            if t is None:
                continue
            se, sv = t
            if sv > 0 and w.get(se, 0) < sv:
                assert sv <= self.cnt[se], ("wait on future token", se, sv, self.cnt[se])
                w[se] = sv
        self.steps[eng].append((w, fn, inc, sem))
        if inc:
            self.cnt[sem] += inc
        return (sem, self.cnt[sem])

    def dma(self, q, fn, waits=(), fam=None):
        fam = fam or q
        r = self.R.get(fam, 0)
        if r:
            i = self.dma_idx.get(fam, 0)
            self.dma_idx[fam] = i + 1
            sem = "%s@%d" % (fam, i % r)
        else:
            sem = fam
        w = list(waits)
        if sem in self.last_dma:
            w.append(self.last_dma[sem])
        tok = self.add(q, fn, w, inc=16, sem=sem)
        self.last_dma[sem] = tok
        return tok

    def tok(self, eng):
        return (eng, self.cnt[eng])

    def bar(self):
        return [(s, c) for s, c in self.cnt.items()
                if not (s.startswith("cv@") or s.startswith("cw@"))]

    def emit(self, blk):
        for eng in _ENG2BLK:
            steps = self.steps[eng]
            sem_all = self.sem

            def body(e, steps=steps):
                hwm = {k: 0 for k in self.cnt}
                for w, fn, inc, sem in steps:
                    for se, sv in w.items():
                        if sv > hwm[se]:
                            e.wait_ge(sem_all[se], sv)
                            hwm[se] = sv
                    inst = fn(e)
                    if inc:
                        inst.then_inc(sem_all[sem], inc)
            getattr(blk, _ENG2BLK[eng])(body)


class TP:

    def __init__(self, S, psTa, psTb, ident):
        self.S, self.ps, self.id = S, (psTa, psTb), ident
        self.n = 0
        self.ev = [None] * 16
        self.last_ev_bank = [None, None]

    def run(self, src_ap, dst_ap, waits, eng):
        slot = self.n % 16
        bank, off = slot % 2, (slot // 2) * 128
        ps = self.ps[bank]
        w = list(waits)
        if self.last_ev_bank[bank] is not None:
            w.append(self.last_ev_bank[bank])
        if self.ev[slot] is not None:
            w.append(self.ev[slot])
        t_tp = self.S.add(
            "pe", lambda e, ps=ps, off=off, src=src_ap:
            e.transpose(ps[:, off:off + 128], src, self.id), waits=w)
        if eng == "act":
            t_ev = self.S.add(
                eng, lambda e, ps=ps, off=off, dst=dst_ap:
                e.copy(dst, ps[:, off:off + 128]), waits=[t_tp])
        else:
            t_ev = self.S.add(
                eng, lambda e, ps=ps, off=off, dst=dst_ap:
                e.tensor_copy(dst, ps[:, off:off + 128]), waits=[t_tp])
        self.ev[slot] = t_ev
        self.last_ev_bank[bank] = t_ev
        self.n += 1
        return t_ev


def _np_consts():
    gsel = np.zeros((128, 32), dtype=np.float32)
    for k in range(128):
        gsel[k, k // 4] = 1.0
    sel127 = np.zeros((32, 128), dtype=np.float32)
    sel127[:, 127] = 1.0
    tokc = ((np.arange(32)[None, :] * 128 + np.arange(128)[:, None]) // 4)
    return gsel, sel127, tokc.astype(np.int32)


def build(stage="full"):
    nc = bass.Bass("TRN2", target_bir_lowering=False)
    dram = nc.dram_tensor

    xTk = dram("xTk", [H, S], F32, kind="ExternalInput").ap()
    xTq = dram("xTq", [H, QTOK], F32, kind="ExternalInput").ap()
    xres = dram("xres", [QTOK, H], F32, kind="ExternalInput").ap()
    wT = {m: dram("w%sT" % m, [H, H], F32, kind="ExternalInput").ap()
          for m in ("q", "k", "v", "o")}
    gateT = dram("gateT", [EXP, H, IDIM], F32, kind="ExternalInput").ap()
    upT = dram("upT", [EXP, H, IDIM], F32, kind="ExternalInput").ap()
    downT = dram("downT", [EXP, IDIM, H], F32, kind="ExternalInput").ap()
    tki = dram("tki", [QTOK, TOPK], I32, kind="ExternalInput").ap()
    tkw = dram("tkw", [QTOK, TOPK], F32, kind="ExternalInput").ap()
    phi = dram("phi", [8192, TOPK], I32, kind="ExternalInput").ap()
    phw = dram("phw", [8192, TOPK], F32, kind="ExternalInput").ap()
    msk = dram("msk", [NT, 128, 512], BF16, kind="ExternalInput").ap()
    g7 = dram("g7", [1, 1], F32, kind="ExternalInput").ap()
    out = dram("out", [QTOK, H], F32, kind="ExternalOutput").ap()

    wbf = dram("wbf", [NCONV, 128, 16 * 1024], BF16, kind="Internal").ap()
    VTd = dram("VTd", [H, S], BF16, kind="Internal").ap()
    aTd = dram("aTd", [NH, 128, QTOK], BF16, kind="Internal").ap()
    x1f = dram("x1f", [QTOK, H], F32, kind="Internal").ap()
    x1b = dram("x1b", [QTOK, H], BF16, kind="Internal").ap()
    tokslots = dram("tokslots", [NSLOT], I32, kind="Internal").ap()
    wslots = dram("wslots", [NSLOT], F32, kind="Internal").ap()
    osc = dram("osc", [NSLOT, H], F32, kind="Internal").ap()

    np_gsel, np_sel127, np_tokc = _np_consts()
    Ci = {
        "idbf": nc.inline_tensor(np.eye(128, dtype=ml_dtypes.bfloat16), "c_idbf").ap(),
        "idf": nc.inline_tensor(np.eye(128, dtype=np.float32), "c_idf").ap(),
        "ut": nc.inline_tensor(np.triu(np.ones((128, 128), np.float32), 1), "c_ut").ap(),
        "gsel": nc.inline_tensor(np_gsel, "c_gsel").ap(),
        "sel127": nc.inline_tensor(np_sel127, "c_sel127").ap(),
        "onescol": nc.inline_tensor(np.ones((128, 1), np.float32), "c_onescol").ap(),
        "ones1row": nc.inline_tensor(np.ones((1, 128), np.float32), "c_ones1row").ap(),
        "capiota": nc.inline_tensor(
            (np.arange(32, dtype=np.float32) * CAP).reshape(32, 1), "c_capiota").ap(),
        "tokc": nc.inline_tensor(np_tokc, "c_tokc").ap(),
        "tok1023": nc.inline_tensor(
            np.full((32, 1), QTOK - 1, np.int32), "c_tok1023").ap(),
    }

    with ExitStack() as ctx:
        _name_n = [0]

        def sb(name, shape, dt, c=ctx):
            _name_n[0] += 1
            return c.enter_context(
                nc.sbuf_tensor("%s_%d" % (name, _name_n[0]), shape, dt))

        idbf = sb("s_idbf", [128, 128], BF16)
        idf = sb("s_idf", [128, 128], F32)
        ut = sb("s_ut", [128, 128], F32)
        gsel = sb("s_gsel", [128, 32], F32)
        sel127 = sb("s_sel127", [32, 128], F32)
        onescol = sb("s_onescol", [128, 1], F32)
        ones1row = sb("s_ones1row", [1, 128], F32)
        capiota = sb("s_capiota", [32, 1], F32)
        tokc = sb("s_tokc", [128, 32], I32)
        tok1023 = sb("s_tok1023", [32, 1], I32)
        masks = sb("s_masks", [128, NT * 512], BF16)

        idxA = sb("idxA", [32, 128], I32)
        idxAf = sb("idxAf", [32, 128], F32)
        wA = sb("wA", [32, 128], F32)
        idxT = sb("idxT", [128, 32], F32)
        wTt = sb("wTt", [128, 32], F32)
        Msb = sb("Msb", [128, 32], F32)
        Rsb = sb("Rsb", [128, 32], F32)
        PREVsb = sb("PREVsb", [128, 32], F32)
        scanb = sb("scanb", [128, 32], F32)
        rankb = sb("rankb", [128, 32], F32)
        geb = sb("geb", [128, 32], F32)
        valb = sb("valb", [128, 32], F32)
        dest = sb("dest", [128, 32], F32)
        dest_i = sb("dest_i", [128, 32], I32)
        zeros32 = sb("zeros32", [128, 32], F32)
        phA = sb("phA", [128, 256], I32)
        phAf = sb("phAf", [128, 256], F32)
        phWf = sb("phWf", [128, 256], F32)
        phT = sb("phT", [128, 256], F32)
        WRsb = sb("WRsb", [128, 32], F32)
        counts = sb("counts", [32, 1], F32)
        Wcol = sb("Wcol", [32, 1], F32)
        g7sb = sb("g7sb", [1, 1], F32)
        g7col = sb("g7col", [32, 1], F32)
        ge2 = sb("ge2", [32, 1], F32)
        dsynf = sb("dsynf", [32, 1], F32)
        dsyn_i = sb("dsyn_i", [32, 1], I32)
        slotsG = sb("slotsG", [64, 128], I32)
        slotsGf = sb("slotsGf", [64, 128], F32)
        toksTf = sb("toksTf", [128, 64], F32)
        toks_i = sb("toks_i", [128, 64], I32)
        wcolsT = sb("wcolsT", [128, 64], F32)
        zcol_f = sb("zcol_f", [128, 64], F32)
        zcol_i = sb("zcol_i", [128, 64], I32)

        negmax_c = sb("negmax_c", [128, 1], F32)
        cmx = sb("cmx", [128, 4], F32)
        bias_c = sb("bias_c", [128, 1], F32)
        den_c = sb("den_c", [128, 1], F32)
        den2_c = sb("den2_c", [128, 1], F32)
        inv_c = sb("inv_c", [128, 1], F32)
        scol = sb("scol", [128, 1], F32)
        nmean = sb("nmean", [128, 1], F32)
        sscol = sb("sscol", [128, 1], F32)
        m2col = sb("m2col", [128, 1], F32)
        varcol = sb("varcol", [128, 1], F32)
        lncol = sb("lncol", [128, 1], F32)
        rstdcol = sb("rstdcol", [128, 1], F32)
        eps_col = sb("eps_col", [128, 1], F32)
        invrow = sb("invrow", [1, 1024], F32)

        psA = ctx.enter_context(nc.psum_tensor("psA", [128, 2048], F32))
        psB = ctx.enter_context(nc.psum_tensor("psB", [128, 1024], F32))
        psTa = ctx.enter_context(nc.psum_tensor("psTa", [128, 1024], BF16))
        psTb = ctx.enter_context(nc.psum_tensor("psTb", [128, 1024], BF16))

        S_ = Sch(nc, ctx)
        blk = ctx.enter_context(nc.Block())
        reg_bnd_ns = nc.alloc_register(mybir.EngineType.Pool, "bnd_ns")
        reg_bnd_tok = nc.alloc_register(mybir.EngineType.Pool, "bnd_tok")

        def _init_regs(e):
            e.reg_mov(reg_bnd_ns, NSLOT - 1)
            return e.reg_mov(reg_bnd_tok, QTOK - 1)
        S_.add("pool", _init_regs, inc=0)
        tp = TP(S_, psTa, psTb, idbf[:])

        conv_tok = {}

        def conv_unit(u, W):
            ei, m = divmod(u, 3)
            toks = []
            for h in range(2):
                if m == 2:
                    src = downT[ei][h * 512:(h + 1) * 512, :].rearrange(
                        "(c p) h2 -> p c h2", p=128)
                    cdim = 4
                elif m == 1:
                    src = upT[ei][h * 1024:(h + 1) * 1024, :].rearrange(
                        "(c p) i -> p c i", p=128)
                    cdim = 8
                else:
                    src = gateT[ei][h * 1024:(h + 1) * 1024, :].rearrange(
                        "(c p) i -> p c i", p=128)
                    cdim = 8
                b = W["n"] % 2
                W["n"] += 1
                stg = W["stg"][:, b * 8192:(b + 1) * 8192]
                lw = [W["free"][b]] if W["free"][b] else list(W["init"])
                t_ld = S_.dma("pool", lambda e, d=stg, s=src, c=cdim:
                              e.dma_start(out=d.rearrange("p (c f) -> p c f", c=c),
                                          in_=s), waits=lw, fam="cv")
                t_st = S_.dma("pool", lambda e, u=u, h=h, s=stg: e.dma_start(
                    out=wbf[u][:, h * 8192:(h + 1) * 8192], in_=s),
                    waits=[t_ld], fam="cw")
                W["free"][b] = t_st
                toks.append(t_st)
            conv_tok[u] = toks

        t_eps = None
        for dst, key in ((idbf, "idbf"), (idf, "idf"), (ut, "ut"),
                         (gsel, "gsel"), (sel127, "sel127"),
                         (onescol, "onescol"), (ones1row, "ones1row"),
                         (capiota, "capiota"), (tokc, "tokc"),
                         (tok1023, "tok1023")):
            S_.dma("sp", lambda e, d=dst, s=Ci[key]: e.dma_start(out=d[:], in_=s))
        S_.dma("sp", lambda e: e.dma_start(
            out=masks[:].rearrange("p (i j) -> p i j", i=NT),
            in_=msk.rearrange("i p j -> p i j")))
        t_eps = S_.add("dve", lambda e: e.memset(eps_col[:], 1e-5))
        t_cl = S_.tok("sp")

        t_tki = S_.dma("sp", lambda e: e.dma_start(
            out=idxA[:].rearrange("A (pa k) -> A pa k", k=TOPK),
            in_=tki.rearrange("(A pa) k -> A pa k", pa=32)))
        t_tkw = S_.dma("sp", lambda e: e.dma_start(
            out=wA[:].rearrange("A (pa k) -> A pa k", k=TOPK),
            in_=tkw.rearrange("(A pa) k -> A pa k", pa=32)))
        t_phi = S_.dma("sp", lambda e: e.dma_start(
            out=phA[:].rearrange("p (r k) -> p r k", k=TOPK),
            in_=phi.rearrange("(p r) k -> p r k", p=128)))
        t_phw = S_.dma("sp", lambda e: e.dma_start(
            out=phWf[:].rearrange("p (r k) -> p r k", k=TOPK),
            in_=phw.rearrange("(p r) k -> p r k", p=128)))
        t_g7 = S_.dma("sp", lambda e: e.dma_start(out=g7sb[:], in_=g7))

        t_if = S_.add("dve", lambda e: e.tensor_copy(idxAf[:], idxA[:]), waits=[t_tki])
        t_pf = S_.add("dve", lambda e: e.tensor_copy(phAf[:], phA[:]), waits=[t_phi])
        t1 = S_.add("pe", lambda e: e.transpose(psB[0:128, 0:32], idxAf[:],
                                                idf[0:32, 0:32]), waits=[t_if, t_cl])
        t1e = S_.add("dve", lambda e: e.tensor_copy(idxT[:], psB[0:128, 0:32]),
                     waits=[t1])
        t2 = S_.add("pe", lambda e: e.transpose(psB[0:128, 32:64], wA[:],
                                                idf[0:32, 0:32]),
                    waits=[t_tkw, t1e])
        t2e = S_.add("dve", lambda e: e.tensor_copy(wTt[:], psB[0:128, 32:64]),
                     waits=[t2])
        t3 = S_.add("pe", lambda e: e.matmul(psB[0:32, 64:65], ones1row[0:1, 0:32],
                                             g7sb[:], start=True, stop=True),
                    waits=[t_g7, t_cl, t2e])
        t3e = S_.add("dve", lambda e: e.tensor_copy(g7col[:], psB[0:32, 64:65]),
                     waits=[t3])

        tR = None
        for e_i in range(EXP):
            tR = S_.add("dve", lambda e, v=float(e_i), e_i=e_i: e.tensor_scalar(
                Msb[:], idxT[:], v, None, ALU.is_equal, ALU.add,
                accum_out=Rsb[:, e_i:e_i + 1]), waits=[t1e, tR])
            tq = S_.add("dve", lambda e, v=float(e_i): e.scalar_tensor_tensor(
                phT[:], phAf[:], v, phWf[:], ALU.is_equal, ALU.mult),
                waits=[t_pf, t_phw, tR])
            tR = S_.add("dve", lambda e, e_i=e_i: e.reduce_sum(
                WRsb[:, e_i:e_i + 1], phT[:], axis=AX.X), waits=[tq])
        t_loop1 = tR

        t_p0_psB = S_.tok("dve")
        P0B = {}

        def emit_p0b():
            emit_p0b1()
            emit_p0b2()

        def emit_p0b1():
            t4 = S_.add("pe", lambda e: e.matmul(psB[0:128, 128:160], ut[:], Rsb[:],
                                                 start=True, stop=True),
                        waits=[t_loop1, t3e])
            t4e = S_.add("dve", lambda e: e.tensor_copy(PREVsb[:], psB[0:128, 128:160]),
                         waits=[t4])
            t5 = S_.add("pe", lambda e: e.matmul(psB[0:32, 512:513], Rsb[:], onescol[:],
                                                 start=True, stop=True), waits=[t_loop1])
            t5e = S_.add("dve", lambda e: e.tensor_copy(counts[:], psB[0:32, 512:513]),
                         waits=[t5])
            t6 = S_.add("pe", lambda e: e.matmul(psB[0:32, 520:521], WRsb[:], onescol[:],
                                                 start=True, stop=True), waits=[t_loop1])
            t6e = S_.add("dve", lambda e: e.tensor_copy(Wcol[:], psB[0:32, 520:521]),
                         waits=[t6])
            t6g = S_.add("dve", lambda e: e.tensor_mul(Wcol[:], Wcol[:], g7col[:]),
                         waits=[t6e, t3e])

            tz0 = S_.add("dve", lambda e: e.memset(zeros32[:], 0.0))
            tda = S_.add("dve", lambda e: e.memset(dest[:], 0.0))

            tL = None
            for e_i in range(EXP):
                tm = S_.add("dve", lambda e, v=float(e_i): e.tensor_single_scalar(
                    Msb[:], idxT[:], v, ALU.is_equal), waits=[t4e, tz0, tda, tL])
                ts = S_.add("dve", lambda e: e.tensor_tensor_scan(
                    scanb[:], Msb[:], zeros32[:], 0.0, ALU.add, ALU.add), waits=[tm])
                ts = S_.add("dve", lambda e, e_i=e_i: e.scalar_tensor_tensor(
                    rankb[:], scanb[:], PREVsb[:, e_i:e_i + 1], Msb[:],
                    ALU.add, ALU.subtract), waits=[ts])
                tg = S_.add("dve", lambda e: e.tensor_single_scalar(
                    geb[:], rankb[:], float(CAP), ALU.is_ge), waits=[ts])
                tv = S_.add("dve", lambda e, v=float(e_i * CAP): e.tensor_scalar_add(
                    valb[:], rankb[:], v), waits=[ts])
                tv = S_.add("dve", lambda e: e.scalar_tensor_tensor(
                    valb[:], geb[:], BIG, valb[:], ALU.mult, ALU.add), waits=[tv, tg])
                tL = S_.add("dve", lambda e: e.tensor_mul(valb[:], valb[:], Msb[:]),
                            waits=[tv])
                tL = S_.add("dve", lambda e: e.tensor_add(dest[:], dest[:], valb[:]),
                            waits=[tL])
            t_dest = S_.add("dve", lambda e: e.tensor_copy(dest_i[:], dest[:]), waits=[tL])

            tg2 = S_.add("dve", lambda e: e.tensor_single_scalar(
                ge2[:], counts[:], float(CAP), ALU.is_ge), waits=[t5e])
            td1 = S_.add("dve", lambda e: e.scalar_tensor_tensor(
                dsynf[:], ge2[:], BIG, counts[:], ALU.mult, ALU.add), waits=[tg2])
            td2 = S_.add("dve", lambda e: e.tensor_add(dsynf[:], dsynf[:], capiota[:]),
                         waits=[td1])
            td3 = S_.add("dve", lambda e: e.tensor_copy(dsyn_i[:], dsynf[:]), waits=[td2])

            tzf = S_.add("dve", lambda e: e.memset(zcol_f[:], 0.0))
            tzi = S_.add("dve", lambda e: e.memset(zcol_i[:], 0))
            tz1 = S_.dma("sp", lambda e: e.dma_start(
                out=wslots.rearrange("(p n) -> p n", p=128), in_=zcol_f[:]), waits=[tzf])
            tz2 = S_.dma("sp", lambda e: e.dma_start(
                out=tokslots.rearrange("(p n) -> p n", p=128), in_=zcol_i[:]), waits=[tzi])


            t_sc = None
            for a in range(32):
                t_sc = S_.dma("pool", lambda e, a=a: e.indirect_dma_start(
                    out=tokslots.rearrange("(n one) -> n one", one=1),
                    out_offset=IndirectOffsetOnAxis(dest_i[:, a:a + 1], 0),
                    in_=tokc[:, a:a + 1], in_offset=None,
                    bounds_check=reg_bnd_ns, oob_is_err=False),
                    waits=[t_dest, tz2, t_cl, t_sc])
                t_sc = S_.dma("pool", lambda e, a=a: e.indirect_dma_start(
                    out=wslots.rearrange("(n one) -> n one", one=1),
                    out_offset=IndirectOffsetOnAxis(dest_i[:, a:a + 1], 0),
                    in_=wTt[:, a:a + 1], in_offset=None,
                    bounds_check=reg_bnd_ns, oob_is_err=False),
                    waits=[tz1, t_sc])
            t_sc = S_.dma("pool", lambda e: e.indirect_dma_start(
                out=tokslots.rearrange("(n one) -> n one", one=1),
                out_offset=IndirectOffsetOnAxis(dsyn_i[:], 0),
                in_=tok1023[:], in_offset=None,
                bounds_check=reg_bnd_ns, oob_is_err=False), waits=[td3, t_sc])
            t_sc = S_.dma("pool", lambda e: e.indirect_dma_start(
                out=wslots.rearrange("(n one) -> n one", one=1),
                out_offset=IndirectOffsetOnAxis(dsyn_i[:], 0),
                in_=Wcol[:], in_offset=None,
                bounds_check=reg_bnd_ns, oob_is_err=False), waits=[t6g, t_sc])
            P0B["sc"] = t_sc
            P0B["t6e"] = t6e
            P0B["dest"] = t_dest
            P0B["td3"] = td3

        def emit_p0b2():
            t_sc = P0B["sc"]
            t6e = P0B["t6e"]
            t_rb1 = S_.dma("sp", lambda e: e.dma_start(
                out=slotsG[:], in_=tokslots.rearrange("(G p) -> G p", p=128)),
                waits=[t_sc])
            t_rb1f = S_.add("dve", lambda e: e.tensor_copy(slotsGf[:], slotsG[:]),
                            waits=[t_rb1])
            t_psA_ok = S_.tok("act")
            t_rt1 = S_.add("pe", lambda e: e.transpose(
                psA[0:128, 0:64], slotsGf[:], idf[0:64, 0:64]),
                waits=[t_rb1f, t6e, t_psA_ok])
            t_rt1e = S_.add("dve", lambda e: e.tensor_copy(
                toksTf[:], psA[0:128, 0:64]), waits=[t_rt1])
            t_toks = S_.add("dve", lambda e: e.tensor_copy(toks_i[:], toksTf[:]),
                            waits=[t_rt1e])
            t_rb2 = S_.dma("sp", lambda e: e.dma_start(
                out=slotsGf[:], in_=wslots.rearrange("(G p) -> G p", p=128)),
                waits=[t_sc, t_rt1])
            t_rt2 = S_.add("pe", lambda e: e.transpose(
                psA[0:128, 64:128], slotsGf[:], idf[0:64, 0:64]),
                waits=[t_rb2, t_rt1e])
            t_wcols = S_.add("dve", lambda e: e.tensor_copy(
                wcolsT[:], psA[0:128, 64:128]), waits=[t_rt2])
            P0B["toks"] = t_toks
            P0B["wcols"] = t_wcols

        if stage == "p0":
            emit_p0b()
            bar_p0 = S_.bar()
            t_outs = []
            t_outs.append(S_.dma("sp", lambda e: e.dma_start(
                out=out[0:64, 0:128], in_=slotsGf[:]), waits=bar_p0))
            t_outs.append(S_.dma("sp", lambda e: e.dma_start(
                out=out[64:192, 0:32], in_=dest[:]), waits=bar_p0))
            t_outs.append(S_.dma("sp", lambda e: e.dma_start(
                out=out[192:320, 0:64], in_=toksTf[:]), waits=bar_p0))
            t_outs.append(S_.dma("sp", lambda e: e.dma_start(
                out=out[320:448, 0:64], in_=wcolsT[:]), waits=bar_p0))
            t_outs.append(S_.dma("sp", lambda e: e.dma_start(
                out=out[448:480, 0:1], in_=Wcol[:],
                allow_slow_non_contiguous=True), waits=bar_p0))
            t_outs.append(S_.dma("sp", lambda e: e.dma_start(
                out=out[480:512, 0:1], in_=dsynf[:],
                allow_slow_non_contiguous=True), waits=bar_p0))
            t_outs.append(S_.dma("sp", lambda e: e.dma_start(
                out=out[512:640, 0:32], in_=idxT[:]), waits=bar_p0))
            t_outs.append(S_.dma("sp", lambda e: e.dma_start(
                out=out[640:672, 0:128], in_=idxAf[:]), waits=bar_p0))
            t_outs.append(S_.dma("sp", lambda e: e.dma_start(
                out=out[672:800, 0:32], in_=Rsb[:]), waits=bar_p0))
            t_outs.append(S_.dma("sp", lambda e: e.dma_start(
                out=out[800:928, 0:32], in_=PREVsb[:]), waits=bar_p0))
            S_.add("sp", lambda e: None, waits=t_outs, inc=0)
            S_.emit(blk)
            return nc

        t_atd = []
        with ExitStack() as attn_ctx:
            KT = sb("KT", [128, HPG * 2048], BF16, attn_ctx)
            QT = sb("QT", [128, HPG * 1024], BF16, attn_ctx)
            KTv = KT[:].rearrange("p (h k) -> p h k", h=HPG)
            QTv = QT[:].rearrange("p (h q) -> p h q", h=HPG)
            mskv = masks[:].rearrange("p (i j) -> p i j", i=NT)

            conv_drain = []
            for g in range((1 if stage == "qkv" else NG)):
                gbar = ([t_cl, t_p0_psB] if g == 0
                        else S_.bar() + list(conv_drain))
                with ExitStack() as p1_ctx:
                    xk = sb("xk", [128, 16 * 2048], BF16, p1_ctx)
                    xq = sb("xq", [128, 16 * 1024], BF16, p1_ctx)
                    wm = sb("wm", [128, 16 * 1024], BF16, p1_ctx)
                    VTst = sb("VTst", [128, 2048], BF16, p1_ctx)
                    xkv = xk[:].rearrange("p (c t) -> p c t", c=16)
                    xqv = xq[:].rearrange("p (c t) -> p c t", c=16)
                    wmv = wm[:].rearrange("p (c d) -> p c d", c=16)

                    t_xk = S_.dma("pool", lambda e: e.dma_start(
                        out=xkv, in_=xTk.rearrange("(c p) t -> p c t", p=128)),
                        waits=gbar)
                    t_xq = S_.dma("pool", lambda e: e.dma_start(
                        out=xqv, in_=xTq.rearrange("(c p) t -> p c t", p=128)),
                        waits=gbar)

                    t_psA_free = list(gbar)
                    t_psB_free = list(gbar)
                    t_vst = None
                    t_wm_free = None
                    mats = ("k", "q") if stage == "qkv" else ("k", "q", "v")
                    for mat in mats:
                        t_wm = S_.dma("pool", lambda e, mat=mat, g=g: e.dma_start(
                            out=wmv,
                            in_=wT[mat][:, g * 1024:(g + 1) * 1024]
                            .rearrange("(c p) d -> p c d", p=128)),
                            waits=gbar + [t_wm_free])
                        kq = 1024 if mat == "q" else 2048
                        xv = xqv if mat == "q" else xkv
                        t_x = t_xq if mat == "q" else t_xk
                        ps = psB if mat == "q" else psA
                        for hd in range(HPG):
                            def mmfn(e, ps=ps, hd=hd, kq=kq, xv=xv):
                                last = None
                                for hc in range(16):
                                    for sl in range(kq // 512):
                                        last = e.matmul(
                                            ps[:, sl * 512:(sl + 1) * 512],
                                            wmv[:, hc, hd * 128:(hd + 1) * 128],
                                            xv[:, hc, sl * 512:(sl + 1) * 512],
                                            start=(hc == 0), stop=(hc == 15))
                                return last
                            pf = t_psB_free if mat == "q" else t_psA_free
                            t_mm = S_.add("pe", mmfn, waits=[t_wm, t_x] + pf)
                            if mat == "k":
                                dst0, dst1 = KTv[:, hd, 0:1024], KTv[:, hd, 1024:2048]
                                half = 1024
                                ew = [t_mm]
                            elif mat == "v":
                                dst0, dst1 = VTst[:, 0:1024], VTst[:, 1024:2048]
                                half = 1024
                                ew = [t_mm, t_vst]
                            else:
                                dst0, dst1 = QTv[:, hd, 0:512], QTv[:, hd, 512:1024]
                                half = 512
                                ew = [t_mm]
                            te0 = S_.add("act", lambda e, ps=ps, d=dst0, h=half:
                                         e.copy(d, ps[:, 0:h]), waits=ew)
                            te1 = S_.add("act", lambda e, ps=ps, d=dst1, h=half:
                                         e.copy(d, ps[:, h:2 * h]), waits=ew)
                            if mat == "q":
                                t_psB_free = [te0, te1]
                            else:
                                t_psA_free = [te0, te1]
                            if mat == "v":
                                gh = g * HPG + hd
                                t_vst = S_.dma("sp", lambda e, gh=gh: e.dma_start(
                                    out=VTd[gh * 128:(gh + 1) * 128, :],
                                    in_=VTst[:]), waits=[te0, te1])
                        t_wm_free = S_.tok("pe")
                    t_p1 = S_.bar()
                if stage == "qkv":
                    t_outs = []
                    t_outs.append(S_.dma("pool", lambda e: e.dma_start(
                        out=out.rearrange("(a p) o -> p a o", p=128),
                        in_=KT[:].rearrange("p (a o) -> p a o", a=8)),
                        waits=t_p1))
                    S_.add("sp", lambda e: None, waits=t_outs, inc=0)
                    S_.emit(blk)
                    return nc

                if g == 0:
                    emit_p0b1()

                with ExitStack() as p2_ctx:
                    Vsb = sb("Vsb", [128, 16 * 1024], BF16, p2_ctx)
                    ET = sb("ET", [128, 16 * 1024], BF16, p2_ctx)
                    Erot = sb("Erot", [128, 2 * 2048], BF16, p2_ctx)
                    Erot2 = sb("Erot2", [128, 2 * 2048], BF16, p2_ctx)
                    at_sb = sb("at_sb", [128, 1024], BF16, p2_ctx)
                    Vv = Vsb[:].rearrange("p (kc d) -> p kc d", kc=16)
                    ETv = ET[:].rearrange("p (kc q) -> p kc q", kc=16)

                    for kc in range(16):
                        S_.dma("sp", lambda e, kc=kc, g=g: e.dma_start(
                            out=Vv[:, kc, :],
                            in_=VTd[g * 1024:(g + 1) * 1024,
                                    kc * 128:(kc + 1) * 128],
                            transpose=True), waits=t_p1)
                    t_vl = S_.tok("sp")

                    cvstg = sb("cvstg", [128, 16384], BF16, p2_ctx)
                    Wc = {"stg": cvstg[:], "n": 0, "free": [None, None],
                          "init": list(t_p1)}
                    for u in range(0 if g == 0 else CV_W0,
                                   CV_W0 if g == 0 else NCONV):
                        conv_unit(u, Wc)
                    conv_drain[:] = [t for t in Wc["free"] if t]

                    t_exp_p7 = None
                    t_atev_pp = None
                    t_atev_p = None
                    prev_et = None
                    prev_esc = None
                    t_atd_prev = None

                    def av_group(e, hd, kcs):
                        last = None
                        for kc in kcs:
                            qlo = _QLO[kc]
                            for sl in range(2):
                                c0 = max(qlo, sl * 512)
                                c1 = (sl + 1) * 512
                                if c0 >= c1:
                                    continue
                                last = e.matmul(
                                    psB[:, c0:c1],
                                    Vv[:, kc, hd * 128:(hd + 1) * 128],
                                    ETv[:, kc, c0:c1],
                                    start=(kc == 0), stop=(kc == 15))
                        return last

                    for hd in range(HPG):
                        gh = g * HPG + hd
                        t_exp = [None] * 8
                        t_den_rd = [None] * 8
                        t_esc = [None] * 8
                        t_lastp = [None] * 8
                        t_av_last = None

                        for i in range(10):
                            if hd > 0 and i < 8:
                                t_av_last = S_.add(
                                    "pe", lambda e, hd=hd, i=i:
                                    av_group(e, hd - 1, [2 * i, 2 * i + 1]),
                                    waits=(prev_et if i == 0 else []) +
                                    ([t_atev_pp, t_vl] if i == 0 else []))
                            if i >= 2:
                                j = i - 2
                                par_j = j % 2
                                for kc in range(4 * _K(j)):
                                    tp.run(
                                        Erot2[:, par_j * 2048 + kc * 128:
                                              par_j * 2048 + (kc + 1) * 128],
                                        ETv[:, kc, j * 128:(j + 1) * 128],
                                        [t_esc[j]], ("dve", "act")[kc % 2])
                                t_lastp[j] = S_.tok("pe")
                            if i >= 8:
                                continue
                            Ki = _K(i)
                            W = Ki * 512
                            off = (i % 2) * 1024 if i < 4 else 0
                            if i < 4:
                                w_ps = [t_exp[i - 2] if i >= 2 else None]
                                if i == 0:
                                    w_ps = ([t_exp_p7] if t_exp_p7
                                            else list(t_p1))
                            elif i == 4:
                                w_ps = [t_exp[2], t_exp[3]]
                            else:
                                w_ps = [t_exp[i - 1]]
                            t_cm = [None] * Ki
                            for ks in range(Ki):
                                t_sck = S_.add(
                                    "pe", lambda e, hd=hd, i=i, ks=ks, off=off:
                                    e.matmul(
                                        psA[:, off + ks * 512:off + (ks + 1) * 512],
                                        QTv[:, hd, i * 128:(i + 1) * 128],
                                        KTv[:, hd, ks * 512:(ks + 1) * 512],
                                        start=True, stop=True),
                                    waits=w_ps if ks == 0 else [])
                                if ks == Ki - 1:
                                    t_sck = S_.add(
                                        "dve", lambda e, i=i, Ki=Ki, off=off:
                                        e.tensor_tensor(
                                            psA[:, off + (Ki - 1) * 512:
                                                off + Ki * 512],
                                            psA[:, off + (Ki - 1) * 512:
                                                off + Ki * 512],
                                            mskv[:, i, :], ALU.add),
                                        waits=[t_sck, t_cl])
                                t_cm[ks] = S_.add(
                                    "dve", lambda e, ks=ks, off=off:
                                    e.reduce_max(
                                        cmx[:, ks:ks + 1],
                                        psA[:, off + ks * 512:off + (ks + 1) * 512],
                                        axis=AX.X), waits=[t_sck])
                            t_mx = S_.add("dve", lambda e, Ki=Ki: e.reduce_max(
                                negmax_c[:], cmx[:, 0:Ki], axis=AX.X,
                                negate=True), waits=t_cm)
                            t_b = S_.add("dve", lambda e: e.tensor_scalar_mul(
                                bias_c[:], negmax_c[:], SCALE), waits=[t_mx])
                            par = i % 2
                            t_exp[i] = S_.add(
                                "act", lambda e, par=par, W=W, off=off:
                                e.activation(
                                    Erot[:, par * 2048:par * 2048 + W],
                                    psA[:, off:off + W], AF.Exp, bias=bias_c[:],
                                    scale=SCALE, accum_out=den_c[:]),
                                waits=[t_b,
                                       t_esc[i - 2] if i >= 2 else None,
                                       t_den_rd[i - 1] if i >= 1 else None])
                            t_den_rd[i] = S_.add(
                                "dve", lambda e: e.tensor_scalar_add(
                                    den2_c[:], den_c[:], 1e-8), waits=[t_exp[i]])
                            t_rc = S_.add(
                                "dve", lambda e: e.reciprocal(
                                    inv_c[:], den2_c[:]),
                                waits=[t_den_rd[i],
                                       t_esc[i - 1] if i >= 1 else None])
                            t_esc[i] = S_.add(
                                "dve", lambda e, par=par, W=W: e.tensor_scalar_mul(
                                    Erot2[:, par * 2048:par * 2048 + W],
                                    Erot[:, par * 2048:par * 2048 + W],
                                    inv_c[:]),
                                waits=[t_rc,
                                       t_lastp[i - 2] if i >= 2 else None])
                        t_exp_p7 = t_exp[7]

                        if hd > 0:
                            t_atev = S_.add("act", lambda e: e.copy(
                                at_sb[:], psB[:, 0:1024]),
                                waits=[t_av_last, t_atd_prev])
                            t_atd_prev = S_.dma(
                                "sp", lambda e, gh=gh - 1: e.dma_start(
                                    out=aTd[gh], in_=at_sb[:]), waits=[t_atev])
                            t_atd.append(t_atd_prev)
                            t_atev_pp = t_atev_p
                            t_atev_p = t_atev
                        prev_et = [S_.tok("dve"), S_.tok("act")]

                    t_av_last = S_.add(
                        "pe", lambda e: av_group(e, HPG - 1, list(range(16))),
                        waits=prev_et + [t_atev_p, t_vl, t_exp_p7])
                    t_atev = S_.add("act", lambda e: e.copy(
                        at_sb[:], psB[:, 0:1024]), waits=[t_av_last, t_atd_prev])
                    t_atd_prev = S_.dma("sp", lambda e, gh=g * HPG + HPG - 1:
                                        e.dma_start(out=aTd[gh], in_=at_sb[:]),
                                        waits=[t_atev])
                    t_atd.append(t_atd_prev)
                if g == 0:
                    emit_p0b2()
            bar_attn = S_.bar() + list(conv_drain)
            if stage == "att":
                t_outs = []
                outv8 = out.rearrange("(g p) q -> g p q", g=8)
                for hh in range(16):
                    t_outs.append(S_.dma("pool", lambda e, hh=hh: e.dma_start(
                        out=outv8[hh // 2][:, (hh % 2) * 1024:(hh % 2 + 1) * 1024],
                        in_=aTd[hh]), waits=bar_attn))
                S_.add("sp", lambda e: None, waits=t_outs, inc=0)
                S_.emit(blk)
                return nc

        with ExitStack() as p3_ctx:
            wo_bf = sb("wo_bf", [128, 16 * 2048], BF16, p3_ctx)
            atT = sb("atT", [128, 16 * 1024], BF16, p3_ctx)
            xr0 = sb("xr0", [128, 2048], F32, p3_ctx)
            xr1 = sb("xr1", [128, 2048], F32, p3_ctx)
            zf = sb("zf", [128, 2048], F32, p3_ctx)
            sq = sb("sq", [128, 2048], F32, p3_ctx)
            x1t = sb("x1t", [128, 2048], F32, p3_ctx)
            wov = wo_bf[:].rearrange("p (c o) -> p c o", c=16)
            atv = atT[:].rearrange("p (c q) -> p c q", c=16)
            xrb = (xr0, xr1)

            t_wo = S_.dma("pool", lambda e: e.dma_start(
                out=wov, in_=wT["o"].rearrange("(c p) o -> p c o", p=128)),
                waits=bar_attn)
            t_atl = S_.dma("sp", lambda e: e.dma_start(
                out=atv, in_=aTd.rearrange("h p q -> p h q")),
                waits=bar_attn + t_atd)

            t_xr = [None] * NT
            t_xr[0] = S_.dma("sp", lambda e: e.dma_start(
                out=xr0[:], in_=xres[0:128, :]), waits=bar_attn)
            t_xr[1] = S_.dma("sp", lambda e: e.dma_start(
                out=xr1[:], in_=xres[128:256, :]), waits=bar_attn)

            t_x1b = [None] * NT
            t_psA_free = list(bar_attn)
            t_x1t_free = None
            t_z_prev = None
            for t in range(NT):
                def wofn(e, t=t):
                    last = None
                    for dc in range(16):
                        for sl in range(4):
                            last = e.matmul(
                                psA[:, sl * 512:(sl + 1) * 512],
                                atv[:, dc, t * 128:(t + 1) * 128],
                                wov[:, dc, sl * 512:(sl + 1) * 512],
                                start=(dc == 0), stop=(dc == 15))
                    return last
                t_wo_mm = S_.add("pe", wofn, waits=[t_wo, t_atl] + t_psA_free)
                t_z = S_.add("dve", lambda e, t=t: e.tensor_add(
                    zf[:], psA[:, 0:2048], xrb[t % 2][:]),
                    waits=[t_wo_mm, t_xr[t], t_x1t_free])
                t_psA_free = [t_z]
                if t + 2 < NT:
                    t_xr[t + 2] = S_.dma("sp", lambda e, t=t: e.dma_start(
                        out=xrb[t % 2][:], in_=xres[(t + 2) * 128:(t + 3) * 128, :]),
                        waits=[t_z])
                ts1 = S_.add("dve", lambda e: e.reduce_sum(
                    scol[:], zf[:], axis=AX.X, negate=True), waits=[t_z])
                ts2 = S_.add("dve", lambda e: e.tensor_scalar_mul(
                    nmean[:], scol[:], 1.0 / 2048), waits=[ts1])
                tq1 = S_.add("dve", lambda e: e.tensor_mul(sq[:], zf[:], zf[:]),
                             waits=[t_z])
                tq2 = S_.add("dve", lambda e: e.reduce_sum(
                    sscol[:], sq[:], axis=AX.X), waits=[tq1])
                tm2 = S_.add("dve", lambda e: e.tensor_mul(
                    m2col[:], nmean[:], nmean[:]), waits=[ts2])
                tv1 = S_.add("dve", lambda e: e.tensor_scalar_mul(
                    varcol[:], sscol[:], 1.0 / 2048), waits=[tq2])
                tv2 = S_.add("dve", lambda e: e.tensor_sub(
                    varcol[:], varcol[:], m2col[:]), waits=[tv1, tm2])
                tl1 = S_.add("act", lambda e: e.activation(
                    lncol[:], varcol[:], AF.Ln, bias=eps_col[:]), waits=[tv2, t_eps])
                tl2 = S_.add("act", lambda e: e.activation(
                    rstdcol[:], lncol[:], AF.Exp, scale=-0.5), waits=[tl1])
                t_x1 = S_.add("dve", lambda e: e.tensor_scalar(
                    x1t[:], zf[:], nmean[:], rstdcol[:], ALU.add, ALU.mult),
                    waits=[tl2, ts2])
                tf1 = S_.dma("sp", lambda e, t=t: e.dma_start(
                    out=x1f[t * 128:(t + 1) * 128, :], in_=x1t[:]), waits=[t_x1])
                t_x1b[t] = S_.dma("pool", lambda e, t=t: e.dma_start(
                    out=x1b[t * 128:(t + 1) * 128, :], in_=x1t[:]), waits=[t_x1])
                t_x1t_free = t_x1b[t]
                t_z_prev = t_z
            bar_p3 = S_.bar()

        if stage == "x1":
            t_outs = []
            for t in range(NT):
                t_outs.append(S_.dma("sp", lambda e, t=t: e.dma_start(
                    out=out[t * 128:(t + 1) * 128, :],
                    in_=x1f[t * 128:(t + 1) * 128, :]), waits=bar_p3))
            S_.add("sp", lambda e: None, waits=t_outs, inc=0)
            S_.emit(blk)
            return nc

        with ExitStack() as p4_ctx:
            wring = sb("wring", [128, 4 * 16 * 1024], BF16, p4_ctx)
            xs = sb("xs", [128, 2 * 2048], BF16, p4_ctx)
            xsT = sb("xsT", [128, 16 * 256], BF16, p4_ctx)
            hmT = sb("hmT", [128, 8 * 256], BF16, p4_ctx)
            silu0 = sb("silu0", [128, 1024], F32, p4_ctx)
            silu1 = sb("silu1", [128, 1024], F32, p4_ctx)
            hmid = sb("hmid", [128, 1024], BF16, p4_ctx)
            o0 = sb("o0", [128, 2048], F32, p4_ctx)
            o1 = sb("o1", [128, 2048], F32, p4_ctx)
            xsv = xs[:].rearrange("p (st h2) -> p st h2", st=2)
            xsTv = xsT[:].rearrange("p (c s) -> p c s", c=16)
            hmTv = hmT[:].rearrange("p (c s) -> p c s", c=8)
            osb = (o0, o1)
            silus = (silu0, silu1)

            def rg(slot):
                return wring[:, slot * 16384:(slot + 1) * 16384].rearrange(
                    "p (c i) -> p c i", c=16)

            def rd(slot):
                return wring[:, slot * 16384:(slot + 1) * 16384].rearrange(
                    "p (c h2) -> p c h2", c=8)

            t_toks = P0B["toks"]
            t_wcols = P0B["wcols"]
            t_osc = []
            slot_free = [None] * 4
            t_hm1_prev = None
            t_si1_prev = None
            t_oev = {}
            t_xsT_done = {0: None, 1: None}

            def ring_load(u, slot):
                if USE_WBF and u < NCONV:
                    return S_.dma("pool", lambda e, u=u, slot=slot: e.dma_start(
                        out=wring[:, slot * 16384:(slot + 1) * 16384],
                        in_=wbf[u]),
                        waits=bar_p3 + [slot_free[slot]] + conv_tok[u])
                ei, m = divmod(u, 3)
                if m == 0:
                    src = gateT[ei].rearrange("(c p) i -> p c i", p=128)
                    dstv = rg(slot)
                elif m == 1:
                    src = upT[ei].rearrange("(c p) i -> p c i", p=128)
                    dstv = rg(slot)
                else:
                    src = downT[ei].rearrange("(c p) h2 -> p c h2", p=128)
                    dstv = rd(slot)
                return S_.dma("pool", lambda e, d=dstv, s=src: e.dma_start(
                    out=d, in_=s), waits=bar_p3 + [slot_free[slot]])

            for ei in range(EXP):
                sg, su, sd = (3 * ei) % 4, (3 * ei + 1) % 4, (3 * ei + 2) % 4
                t_dg = ring_load(3 * ei, sg)
                t_du = ring_load(3 * ei + 1, su)
                t_dd = ring_load(3 * ei + 2, sd)

                t_gx = [None, None]
                for st in range(2):
                    col = ei * 2 + st
                    t_gx[st] = S_.dma("pool", lambda e, st=st, col=col:
                                      e.indirect_dma_start(
                                          out=xsv[:, st, :], out_offset=None,
                                          in_=x1b, in_offset=IndirectOffsetOnAxis(
                                              toks_i[:, col:col + 1], 0),
                                          bounds_check=reg_bnd_tok,
                                          oob_is_err=False),
                                      waits=[t_toks, t_xsT_done[st]] + t_x1b)
                for st in range(2):
                    for hc in range(16):
                        tp.run(xsv[:, st, hc * 128:(hc + 1) * 128],
                               xsTv[:, hc, st * 128:(st + 1) * 128],
                               [t_gx[st]], ("dve", "act")[hc % 2])
                    t_xsT_done[st] = S_.tok("pe")
                t_xsT_w = [S_.tok("dve"), S_.tok("act")]

                def g_mm(e, st, slot):
                    last = None
                    for hc in range(16):
                        for sl in range(2):
                            last = e.matmul(
                                psB[:, sl * 512:(sl + 1) * 512],
                                xsTv[:, hc, st * 128:(st + 1) * 128],
                                rg(slot)[:, hc, sl * 512:(sl + 1) * 512],
                                start=(hc == 0), stop=(hc == 15))
                    return last

                def u_mm(e, st, slot):
                    last = None
                    for hc in range(16):
                        for sl in range(2):
                            last = e.matmul(
                                psA[:, sl * 512:(sl + 1) * 512],
                                xsTv[:, hc, st * 128:(st + 1) * 128],
                                rg(slot)[:, hc, sl * 512:(sl + 1) * 512],
                                start=(hc == 0), stop=(hc == 15))
                    return last

                oev_p = t_oev.get((ei - 1, 1), (None, None))
                t_g0 = S_.add("pe", lambda e, sg=sg: g_mm(e, 0, sg),
                              waits=t_xsT_w + [t_dg,
                                               t_si1_prev if ei else None])
                t_s0 = S_.add("act", lambda e: e.activation(
                    silu0[:], psB[:, 0:1024], AF.Silu), waits=[t_g0])
                t_u0 = S_.add("pe", lambda e, su=su: u_mm(e, 0, su),
                              waits=[t_du, t_hm1_prev, oev_p[0], oev_p[1]])
                t_g1 = S_.add("pe", lambda e, sg=sg: g_mm(e, 1, sg),
                              waits=[t_s0])
                t_s1 = S_.add("act", lambda e: e.activation(
                    silu1[:], psB[:, 0:1024], AF.Silu), waits=[t_g1])
                t_si1_prev = t_s1
                t_h0 = S_.add("dve", lambda e, col=2 * ei: e.scalar_tensor_tensor(
                    hmid[:], psA[:, 0:1024], wcolsT[:, col:col + 1],
                    silu0[:], ALU.mult, ALU.mult),
                    waits=[t_s0, t_u0, t_wcols])
                for ic in range(8):
                    tp.run(hmid[:, ic * 128:(ic + 1) * 128],
                           hmTv[:, ic, 0:128], [t_h0], ("dve", "act")[ic % 2])
                t_hmT0 = [S_.tok("pe"), S_.tok("dve"), S_.tok("act")]
                t_u1 = S_.add("pe", lambda e, su=su: u_mm(e, 1, su),
                              waits=[t_h0])
                t_h1 = S_.add("dve", lambda e, col=2 * ei + 1: e.scalar_tensor_tensor(
                    hmid[:], psA[:, 0:1024], wcolsT[:, col:col + 1],
                    silu1[:], ALU.mult, ALU.mult),
                    waits=[t_s1, t_u1, t_hmT0[1], t_hmT0[2]])
                t_hm1_prev = t_h1
                for ic in range(8):
                    tp.run(hmid[:, ic * 128:(ic + 1) * 128],
                           hmTv[:, ic, 128:256], [t_h1], ("dve", "act")[ic % 2])
                t_hmT1 = [S_.tok("pe"), S_.tok("dve"), S_.tok("act")]

                def d_mm(e, st, slot):
                    last = None
                    for ic in range(8):
                        for sl in range(4):
                            last = e.matmul(
                                psA[:, sl * 512:(sl + 1) * 512],
                                hmTv[:, ic, st * 128:(st + 1) * 128],
                                rd(slot)[:, ic, sl * 512:(sl + 1) * 512],
                                start=(ic == 0), stop=(ic == 7))
                    return last

                t_d0 = S_.add("pe", lambda e, sd=sd: d_mm(e, 0, sd),
                              waits=t_hmT0[1:] + [t_dd, t_h1,
                                                  oev_p[0], oev_p[1]])
                te0 = S_.add("dve", lambda e: e.tensor_copy(
                    o0[:, 0:1024], psA[:, 0:1024]), waits=[t_d0])
                te1 = S_.add("act", lambda e: e.copy(
                    o0[:, 1024:2048], psA[:, 1024:2048]), waits=[t_d0])
                t_osc.append(S_.dma("sp", lambda e, row0=ei * CAP: e.dma_start(
                    out=osc[row0:row0 + 128, :], in_=o0[:]), waits=[te0, te1]))
                t_d1 = S_.add("pe", lambda e, sd=sd: d_mm(e, 1, sd),
                              waits=t_hmT1[1:] + [te0, te1])
                tf0 = S_.add("dve", lambda e: e.tensor_copy(
                    o1[:, 0:1024], psA[:, 0:1024]), waits=[t_d1])
                tf1 = S_.add("act", lambda e: e.copy(
                    o1[:, 1024:2048], psA[:, 1024:2048]), waits=[t_d1])
                t_oev[(ei, 1)] = (tf0, tf1)
                t_osc.append(S_.dma("sp", lambda e, row0=ei * CAP + 128:
                                    e.dma_start(out=osc[row0:row0 + 128, :],
                                                in_=o1[:]), waits=[tf0, tf1]))
                slot_free[sg] = t_g1
                slot_free[su] = t_u1
                slot_free[sd] = t_d1
            bar_p4 = S_.bar()

        with ExitStack() as p5_ctx:
            G = [[sb("G%d_%d" % (pp, c), [128, 2048], F32, p5_ctx)
                  for c in range(4)] for pp in range(2)]
            G2 = sb("G2", [32, 2048], F32, p5_ctx)
            x1w0 = sb("x1w0", [128, 2048], F32, p5_ctx)
            x1w1 = sb("x1w1", [128, 2048], F32, p5_ctx)
            z2 = sb("z2", [128, 2048], F32, p5_ctx)
            sq2 = sb("sq2", [128, 2048], F32, p5_ctx)
            outt = sb("outt", [128, 2048], F32, p5_ctx)
            x1wb = (x1w0, x1w1)

            t_g2z = S_.add("dve", lambda e: e.memset(G2[:], 0.0), waits=bar_p4)
            t_g2 = S_.dma("pool", lambda e: e.indirect_dma_start(
                out=G2[:], out_offset=None,
                in_=osc, in_offset=IndirectOffsetOnAxis(dsyn_i[:], 0),
                bounds_check=reg_bnd_ns, oob_is_err=False),
                waits=bar_p4 + [t_g2z, P0B["td3"]] + t_osc)

            t_x1r = [None] * NT
            t_x1r[0] = S_.dma("sp", lambda e: e.dma_start(
                out=x1w0[:], in_=x1f[0:128, :]), waits=bar_p4)
            t_x1r[1] = S_.dma("sp", lambda e: e.dma_start(
                out=x1w1[:], in_=x1f[128:256, :]), waits=bar_p4)

            t_outw = []
            t_psA_free = list(bar_p4)
            t_out_prev = None
            t_o_prev = None
            t_gath = [None] * NT
            t_cmb_hist = [None] * NT

            def emit_gathers(t):
                par = t % 2
                toks = []
                prev_cmb = t_cmb_hist[t - 2] if t >= 2 else None
                for c in range(4):
                    a = t * 4 + c
                    tzg = S_.add("dve", lambda e, par=par, c=c: e.memset(
                        G[par][c][:], 0.0),
                        waits=bar_p4 + ([prev_cmb] if prev_cmb else []))
                    toks.append(S_.dma("pool", lambda e, par=par, c=c, a=a:
                                       e.indirect_dma_start(
                                           out=G[par][c][:], out_offset=None,
                                           in_=osc,
                                           in_offset=IndirectOffsetOnAxis(
                                               dest_i[:, a:a + 1], 0),
                                           bounds_check=reg_bnd_ns,
                                           oob_is_err=False),
                                       waits=[tzg, P0B["dest"]] + t_osc))
                t_gath[t] = toks

            emit_gathers(0)
            emit_gathers(1)
            for t in range(NT):
                par = t % 2

                def cmbfn(e, t=t, par=par):
                    last = None
                    for c in range(4):
                        for sl in range(4):
                            last = e.matmul(
                                psA[32 * c:32 * (c + 1), sl * 512:(sl + 1) * 512],
                                gsel[:], G[par][c][:, sl * 512:(sl + 1) * 512],
                                start=True, stop=(t != NT - 1 and c == 3),
                                tile_position=(0, 32 * c))
                    if t == NT - 1:
                        for sl in range(4):
                            last = e.matmul(
                                psA[0:128, sl * 512:(sl + 1) * 512],
                                sel127[:], G2[:, sl * 512:(sl + 1) * 512],
                                start=False, stop=True)
                    return last
                t_cmb = S_.add("pe", cmbfn,
                               waits=t_gath[t] + t_psA_free +
                               ([t_g2] if t == NT - 1 else []))
                t_cmb_hist[t] = t_cmb
                if t + 2 < NT:
                    emit_gathers(t + 2)
                t_z2 = S_.add("dve", lambda e, t=t: e.tensor_add(
                    z2[:], psA[:, 0:2048], x1wb[t % 2][:]),
                    waits=[t_cmb, t_x1r[t], t_o_prev])
                t_psA_free = [t_z2]
                if t + 2 < NT:
                    t_x1r[t + 2] = S_.dma("sp", lambda e, t=t: e.dma_start(
                        out=x1wb[t % 2][:],
                        in_=x1f[(t + 2) * 128:(t + 3) * 128, :]), waits=[t_z2])
                ts1 = S_.add("dve", lambda e: e.reduce_sum(
                    scol[:], z2[:], axis=AX.X, negate=True), waits=[t_z2])
                ts2 = S_.add("dve", lambda e: e.tensor_scalar_mul(
                    nmean[:], scol[:], 1.0 / 2048), waits=[ts1])
                tq1 = S_.add("dve", lambda e: e.tensor_mul(sq2[:], z2[:], z2[:]),
                             waits=[t_z2])
                tq2 = S_.add("dve", lambda e: e.reduce_sum(
                    sscol[:], sq2[:], axis=AX.X), waits=[tq1])
                tm2 = S_.add("dve", lambda e: e.tensor_mul(
                    m2col[:], nmean[:], nmean[:]), waits=[ts2])
                tv1 = S_.add("dve", lambda e: e.tensor_scalar_mul(
                    varcol[:], sscol[:], 1.0 / 2048), waits=[tq2])
                tv2 = S_.add("dve", lambda e: e.tensor_sub(
                    varcol[:], varcol[:], m2col[:]), waits=[tv1, tm2])
                tl1 = S_.add("act", lambda e: e.activation(
                    lncol[:], varcol[:], AF.Ln, bias=eps_col[:]), waits=[tv2, t_eps])
                tl2 = S_.add("act", lambda e: e.activation(
                    rstdcol[:], lncol[:], AF.Exp, scale=-0.5), waits=[tl1])
                t_o = S_.add("dve", lambda e: e.tensor_scalar(
                    outt[:], z2[:], nmean[:], rstdcol[:], ALU.add, ALU.mult),
                    waits=[tl2, ts2, t_out_prev])
                t_o_prev = t_o
                t_out_prev = S_.dma("sp", lambda e, t=t: e.dma_start(
                    out=out[t * 128:(t + 1) * 128, :], in_=outt[:]), waits=[t_o])
                t_outw.append(t_out_prev)

            S_.add("sp", lambda e: None, waits=t_outw, inc=0)
        S_.emit(blk)
    return nc


_CACHE = {}


def _get_nc(stage="full"):
    if stage not in _CACHE:
        _CACHE[stage] = build(stage)
    return _CACHE[stage]


def _rows_of_core(c):
    h = c % 2
    return ((2 * np.arange(8)[:, None] + h) * 128 +
            np.arange(128)[None, :]).reshape(-1)


def _make_masks(h):
    m = np.zeros((NT, 128, 512), dtype=np.float32)
    for i in range(NT):
        Ki = _K(i)
        qa = (2 * i + h) * 128
        kglob = (Ki - 1) * 512 + np.arange(512)[None, :]
        allowed = kglob <= (qa + np.arange(128)[:, None])
        m[i] = np.where(allowed, 0.0, NEG)
    return m.astype(ml_dtypes.bfloat16)


def _prep_host(inputs):
    x = np.ascontiguousarray(np.asarray(inputs["L0_attn_x"], dtype=np.float32))
    wq = np.asarray(inputs["L0_attn_wq"], dtype=np.float32)
    wk = np.asarray(inputs["L0_attn_wk"], dtype=np.float32)
    wv = np.asarray(inputs["L0_attn_wv"], dtype=np.float32)
    wo = np.asarray(inputs["L0_attn_wo"], dtype=np.float32)
    gate = np.asarray(inputs["L0_ffn_gate_weight"], dtype=np.float32)
    up = np.asarray(inputs["L0_ffn_up_weight"], dtype=np.float32)
    down = np.asarray(inputs["L0_ffn_down_weight"], dtype=np.float32)
    tidx = np.asarray(inputs["L0_ffn_topk_idx"], dtype=np.int32)
    tw = np.asarray(inputs["L0_ffn_topk_weight"], dtype=np.float32)

    shared = {
        "wqT": np.ascontiguousarray(wq.T),
        "wkT": np.ascontiguousarray(wk.T),
        "wvT": np.ascontiguousarray(wv.T),
        "woT": np.ascontiguousarray(wo.T),
        "gateT": np.ascontiguousarray(gate.transpose(0, 2, 1)),
        "upT": np.ascontiguousarray(up.transpose(0, 2, 1)),
        "downT": np.ascontiguousarray(down.transpose(0, 2, 1)),
        "phi": np.ascontiguousarray(tidx[8192:16384]),
        "phw": np.ascontiguousarray(tw[8192:16384]),
    }
    xT = np.ascontiguousarray(x.transpose(0, 2, 1))
    in_maps = []
    for c in range(8):
        b, h = c // 2, c % 2
        rows = _rows_of_core(c)
        m = dict(shared)
        m["xTk"] = xT[b]
        m["xTq"] = np.ascontiguousarray(xT[b][:, rows])
        m["xres"] = np.ascontiguousarray(x[b][rows])
        m["tki"] = np.ascontiguousarray(tidx[b * S + rows])
        m["tkw"] = np.ascontiguousarray(tw[b * S + rows])
        m["msk"] = _make_masks(h)
        m["g7"] = np.array([[1.0 if c == 7 else 0.0]], dtype=np.float32)
        in_maps.append(m)
    return in_maps


def kernel(**inputs):
    nc = _get_nc("full")
    in_maps = _prep_host(inputs)
    res = run_bass_kernel_spmd(nc, in_maps, core_ids=list(range(8)))
    outa = np.empty((B, S, H), dtype=np.float32)
    for c in range(8):
        b = c // 2
        rows = _rows_of_core(c)
        outa[b, rows] = np.asarray(res.results[c]["out"])
    return outa
